# version 28
# baseline (speedup 1.0000x reference)
import numpy as np
import contextlib
from collections import defaultdict

import concourse.bass as bass
import concourse.mybir as mybir
from concourse.bass_utils import run_bass_kernel_spmd

F32 = mybir.dt.float32
BF16 = mybir.dt.bfloat16
AF = mybir.ActivationFunctionType
ALU = mybir.AluOpType
AX = mybir.AxisListType

T, C, H, HD, E = 4096, 1024, 16, 64, 8
NCORE = 8
TTILE = 1024
NTILE = T // TTILE
NCH = TTILE // 128
NSLOT = 5
EPS = 1e-5
GC = 0.36787944117144233
GL = 0.21231792754821915
GU = 0.5

ESEM = {"vector": "sDVE", "scalar": "sACT", "tensor": "sPE", "gpsimd": "sGP"}


class Sched:

    def __init__(self):
        self.steps = []
        self.cnt = defaultdict(int)
        self.last_w = {}
        self.readers = defaultdict(dict)

    def _merge(self, waits, tok):
        sem, val = tok
        if val > waits.get(sem, 0):
            waits[sem] = val

    def emit(self, eng, fn, rd=(), wr=(), extra=(), dma_sem=None, self_sync=True, bare=False):
        waits = {}
        for (s, v) in extra:
            self._merge(waits, (s, v))
        for r in rd:
            w = self.last_w.get(r)
            if w is not None and w[0] != eng:
                self._merge(waits, (w[1], w[2]))
        for r in wr:
            w = self.last_w.get(r)
            if w is not None and w[0] != eng:
                self._merge(waits, (w[1], w[2]))
            for k, tok in self.readers.get(r, {}).items():
                if k != eng:
                    self._merge(waits, tok)
        if bare:
            self.steps.append((eng, fn, sorted(waits.items()), None, 0))
            return None
        if dma_sem is None:
            sem = ESEM[eng]
            if self_sync and eng in ("vector", "scalar", "gpsimd") and self.cnt[sem] > 0:
                need = False
                for r in rd:
                    w = self.last_w.get(r)
                    if w is not None and w[0] == eng:
                        need = True
                for r in wr:
                    w = self.last_w.get(r)
                    if w is not None and w[0] == eng:
                        need = True
                    if eng in self.readers.get(r, {}):
                        need = True
                if need:
                    self._merge(waits, (sem, self.cnt[sem]))
            self.cnt[sem] += 1
            tok_after = (sem, self.cnt[sem])
            inc, inc_n = sem, 1
            key = eng
        else:
            self.cnt[dma_sem] += 16
            tok_after = (dma_sem, self.cnt[dma_sem])
            inc, inc_n = dma_sem, 16
            key = "#" + dma_sem
        self.steps.append((eng, fn, sorted(waits.items()), inc, inc_n))
        for r in wr:
            self.last_w[r] = (key, tok_after[0], tok_after[1])
            self.readers[r] = {}
        for r in rd:
            self.readers[r][key] = tok_after
        return tok_after

    def tok(self, sem):
        return (sem, self.cnt[sem])


def build_nc():
    nc = bass.Bass("TRN2", target_bir_lowering=False, debug=False)

    d_xfm = nc.dram_tensor("xfm", [C, T], F32, kind="ExternalInput").ap()
    d_xtm = nc.dram_tensor("xtm", [T, C], F32, kind="ExternalInput").ap()
    d_wk = nc.dram_tensor("wkT", [C, C], F32, kind="ExternalInput").ap()
    d_wv = nc.dram_tensor("wvT", [C, C], F32, kind="ExternalInput").ap()
    d_wr = nc.dram_tensor("wrT", [C, C], F32, kind="ExternalInput").ap()
    d_wo = nc.dram_tensor("woT", [C, C], F32, kind="ExternalInput").ap()
    d_wg = [nc.dram_tensor(f"wg{e}", [C, C], F32, kind="ExternalInput").ap() for e in range(E)]
    d_wu = [nc.dram_tensor(f"wu{e}", [C, C], F32, kind="ExternalInput").ap() for e in range(E)]
    d_wd = [nc.dram_tensor(f"wd{e}", [C, C], F32, kind="ExternalInput").ap() for e in range(E)]
    d_wrf = nc.dram_tensor("wrf", [128, 8, 8], F32, kind="ExternalInput").ap()
    d_mk = nc.dram_tensor("mk", [128, 8], F32, kind="ExternalInput").ap()
    d_mv = nc.dram_tensor("mv", [128, 8], F32, kind="ExternalInput").ap()
    d_mr = nc.dram_tensor("mr", [128, 8], F32, kind="ExternalInput").ap()
    d_td = nc.dram_tensor("td", [1, 16], F32, kind="ExternalInput").ap()
    d_tf = nc.dram_tensor("tf", [1, 16], F32, kind="ExternalInput").ap()
    d_temp = nc.dram_tensor("temp", [1, 1], F32, kind="ExternalInput").ap()
    d_dmat = nc.dram_tensor("dmat", [128, 128], F32, kind="ExternalInput").ap()
    d_mask = nc.dram_tensor("dmask", [128, 128], F32, kind="ExternalInput").ap()
    d_ior = nc.dram_tensor("ior", [1, 128], F32, kind="ExternalInput").ap()
    d_ioc = nc.dram_tensor("ioc", [128, 1], F32, kind="ExternalInput").ap()
    d_idn = nc.dram_tensor("idn", [128, 128], F32, kind="ExternalInput").ap()
    d_out = nc.dram_tensor("out", [T, C], F32, kind="ExternalOutput").ap()

    def wsrc(ap2d):
        return ap2d.rearrange("(c p) n -> p c n", p=128)

    stream = []
    for _tl in range(NTILE):
        stream += [wsrc(d_wk), wsrc(d_wv), wsrc(d_wr), wsrc(d_wo)]
        for e in range(E):
            stream += [wsrc(d_wg[e]), wsrc(d_wu[e]), wsrc(d_wd[e])]
    NW = len(stream)

    S = Sched()

    with contextlib.ExitStack() as X:
        def sb(name, shape, dt=F32):
            return X.enter_context(nc.sbuf_tensor(name, shape, dt))

        def ps(name, shape, dt=F32):
            return X.enter_context(nc.psum_tensor(name, shape, dt))

        slots = sb("slots", [128, NSLOT, 8, 1024], BF16)
        arena = sb("arena", [128, 8 * 1024], F32)
        xlast = sb("xlast", [128, 8, 1], F32)
        yTM = sb("yTM", [128, NCH, 1024], BF16)
        yFM = sb("yFM", [128, 8, 1024], BF16)
        h16 = sb("h16", [128, 8, 512], BF16)
        ek = sb("ek", [128, 1024], F32)
        rhs65 = sb("rhs65", [128, 2, 16, 65], BF16)
        eks = sb("eks", [128, 2, 16], F32)
        rbuf = sb("rbuf", [128, 1024], F32)
        sc1 = sb("sc1", [128, 1024], F32)
        rw16 = sb("rw16", [128, 1024], BF16)
        g_s = sb("g_s", [128, 2, 512], BF16)
        xtmb = sb("xtmb", [128, 2, 1024], F32)
        outb = sb("outb", [128, 1024], F32)
        sc2 = sb("sc2", [128, 1024], F32)
        zroute = sb("zroute", [128, NCH, 8], F32)
        wts = sb("wts", [128, NCH, 8], F32)
        rr = [sb(f"rr{i}", [128, 64], F32) for i in range(5)]
        rs8a = sb("rs8a", [128, 8], F32)
        rs8b = sb("rs8b", [128, 8], F32)
        rne = sb("rne", [128, 8], F32)
        msum = sb("msum", [128, 1], F32)
        negm = sb("negm", [128, 1], F32)
        vsum = sb("vsum", [128, 1], F32)
        tln = sb("tln", [128, 1], F32)
        rstd = sb("rstd", [128, 1], F32)
        nmr = sb("nmr", [128, 1], F32)
        negmB = sb("negmB", [128, 1], F32)
        rstdB = sb("rstdB", [128, 1], F32)
        tri16 = sb("tri16", [128, 16, 128], BF16)
        idn16 = sb("idn16", [128, 128], BF16)
        dvec16 = sb("dvec16", [1, 16 * 128], BF16)
        rvec16 = sb("rvec16", [128, 16], BF16)
        eL16 = sb("eL16", [1, 16], BF16)
        wrow = sb("wrow", [1, 16], F32)
        eurow = sb("eurow", [1, 16], F32)
        wbc = sb("wbc", [128, 16], F32)
        eubc = sb("eubc", [128, 16], F32)
        nrtc = sb("nrtc", [128, 1], F32)
        mk1c = sb("mk1c", [128, 8], F32)
        mv1c = sb("mv1c", [128, 8], F32)
        mr1c = sb("mr1c", [128, 8], F32)
        tempr = sb("tempr", [1, 1], F32)
        rtmp1 = sb("rtmp1", [1, 1], F32)
        onesf = sb("onesf", [1, 128], F32)
        onec = sb("onec", [128, 1], F32)
        epsc = sb("epsc", [128, 1], F32)
        ngcc = sb("ngcc", [128, 1], F32)
        S16 = sb("S16", [1, 1024 + 16], BF16)
        wrf16 = sb("wrf16", [128, 8, 8], BF16)
        xk16 = sb("xk16", [128, 8, 128], BF16)
        xv16 = sb("xv16", [128, 8, 128], BF16)
        xr16 = sb("xr16", [128, 8, 128], BF16)

        pA = ps("pA", [128, 1024], F32)
        pB = ps("pB", [128, 1024], F32)
        pN = ps("pN", [128, 1024], F32)
        pS = ps("pS", [128, 512], F32)
        pT = ps("pT", [128, 1024], BF16)

        sems = {n: X.enter_context(nc.semaphore(n)) for n in
                ("sPE", "sDVE", "sACT", "sGP", "dO", "dM", "dF")
                + tuple(f"dW{i}" for i in range(NSLOT))
                + tuple(f"dX{i}" for i in range(8))
                + tuple(f"dXB{i}" for i in range(8))
                + tuple(f"dXA{i}" for i in range(4, 8)) + ("dT0", "dT1")}

        xfmt = arena[:].rearrange("p (c n) -> p c n", n=1024)
        accv = xfmt
        ARAA = tuple(f"arAA{i}" for i in range(8))
        ARAB = tuple(f"arAB{i}" for i in range(8))
        ARBH = tuple(f"arB{i}" for i in range(8))
        ARB = ARAA + ARAB + ARBH

        def wslotap(m):
            return slots[:, m % NSLOT]

        def hreg(hh):
            if hh < 7:
                return pN, hh * 65
            if hh < 14:
                return pN, 512 + (hh - 7) * 65
            return pS, 64 + (hh - 14) * 65

        YFM = [f"yFM{i}" for i in range(8)]

        def pe(fn, rd=(), wr=()):
            S.emit("tensor", fn, rd, wr)

        def dv(fn, rd=(), wr=()):
            S.emit("vector", fn, rd, wr)

        def ac(fn, rd=(), wr=()):
            S.emit("scalar", fn, rd, wr)

        def gp(fn, rd=(), wr=()):
            S.emit("gpsimd", fn, rd, wr)

        def emit_wdma(m):
            s = m % NSLOT
            src = stream[m]

            def fn(eng, s=s, src=src):
                return eng.dma_start(out=slots[:, s], in_=src)
            S.emit("gpsimd", fn, rd=(), wr=(f"slot{s}",), dma_sem=f"dW{s}")

        def sload(dst_ap, src_ap, res):
            def fn(eng):
                return eng.dma_start(out=dst_ap, in_=src_ap)
            S.emit("sync", fn, rd=(), wr=(res,), dma_sem="dM")

        mkc = rbuf[:, 896:904]
        mvc = rbuf[:, 904:912]
        mrc = rbuf[:, 912:920]
        for jc in range(8):
            def fn_xfm0a(eng, jc=jc):
                return eng.dma_start(out=xfmt[:, jc, 0:512],
                                     in_=d_xfm[jc * 128:(jc + 1) * 128, 0:512])
            S.emit("sync", fn_xfm0a, rd=(), wr=(f"arAA{jc}", f"arAB{jc}"), dma_sem=f"dX{jc}")

            def fn_xfm0b(eng, jc=jc):
                return eng.dma_start(out=xfmt[:, jc, 512:1024],
                                     in_=d_xfm[jc * 128:(jc + 1) * 128, 512:1024])
            S.emit("scalar", fn_xfm0b, rd=(), wr=(f"arB{jc}",), dma_sem=f"dXB{jc}")

        sload(mkc, d_mk, "rbuf")
        sload(mvc, d_mv, "rbuf")
        sload(mrc, d_mr, "rbuf")
        tdr = rbuf[0:1, 512:528]
        tfr = rbuf[0:1, 528:544]
        wtmp = rbuf[0:1, 544:560]
        sload(tdr, d_td, "rbuf")
        sload(tfr, d_tf, "rbuf")
        sload(tempr[:], d_temp, "tempr")
        sload(sc1[:, 0:128], d_dmat, "sc1")
        sload(ek[:, 0:128], d_mask, "ek")
        sload(rbuf[0:1, 0:128], d_ior, "rbuf")
        sload(rbuf[:, 128:129], d_ioc, "rbuf")
        sload(rbuf[:, 256:384], d_idn, "rbuf")

        _full = S.cnt["dM"]
        for r in ("tempr", "sc1", "ek", "rbuf"):
            k, sem, _v = S.last_w[r]
            S.last_w[r] = (k, sem, _full)

        for g in (0, 1):
            def fn_xtm(eng, g=g):
                return eng.dma_start(out=xtmb[:, g % 2, :], in_=d_xtm[g * 128:(g + 1) * 128, :])
            S.emit("sync", fn_xtm, rd=(), wr=(f"xtmb{g % 2}",), dma_sem=f"dT{g % 2}")

        def fn_wrf(eng):
            return eng.dma_start(out=wrf16[:], in_=d_wrf)
        S.emit("gpsimd", fn_wrf, rd=(), wr=("wrf16",), dma_sem="dF")
        for m in range(NSLOT):
            emit_wdma(m)

        def fn_mix1(eng):
            eng.tensor_scalar(out=mk1c[:], in0=mkc, scalar1=1.0, scalar2=None, op0=ALU.subtract)
            eng.tensor_scalar(out=mv1c[:], in0=mvc, scalar1=1.0, scalar2=None, op0=ALU.subtract)
            return eng.tensor_scalar(out=mr1c[:], in0=mrc, scalar1=1.0, scalar2=None, op0=ALU.subtract)
        dv(fn_mix1, rd=("rbuf",), wr=("mk1c", "mv1c", "mr1c"))

        def fn_memset(eng):
            eng.memset(onesf[:], 1.0)
            eng.memset(onec[:], 1.0)
            eng.memset(epsc[:], EPS)
            eng.memset(ngcc[:], -GC)
            eng.memset(S16[:], 0.0)
            eng.memset(xlast[:], 0.0)
            return eng.tensor_copy(idn16[:], rbuf[:, 256:384])
        dv(fn_memset, rd=("rbuf",), wr=("onesf", "onec", "epsc", "ngcc", "S16", "xlast", "idn16"))

        def fn_exp_td(eng):
            eng.activation(out=wtmp, in_=tdr, func=AF.Exp)
            return eng.activation(out=eurow[:], in_=tfr, func=AF.Exp)
        ac(fn_exp_td, rd=("rbuf",), wr=("rbuf", "eurow"))

        def fn_wneg(eng):
            eng.tensor_scalar(out=wrow[:], in0=wtmp, scalar1=-1.0, scalar2=None, op0=ALU.mult)
            eng.drain()
            eng.reciprocal(rtmp1[:], tempr[:])
            eng.drain()
            return eng.tensor_scalar(out=rtmp1[:], in0=rtmp1[:], scalar1=-1.0, scalar2=None, op0=ALU.mult)
        dv(fn_wneg, rd=("rbuf", "tempr"), wr=("wrow", "rtmp1"))

        def fn_bcast_mm(eng):
            eng.matmul(pS[:, 0:16], onesf[:], wrow[:], start=True, stop=True)
            eng.matmul(pS[:, 16:32], onesf[:], eurow[:], start=True, stop=True)
            return eng.matmul(pS[:, 32:33], onesf[:], rtmp1[:], start=True, stop=True)
        pe(fn_bcast_mm, rd=("onesf", "wrow", "eurow", "rtmp1"), wr=("pS",))

        def fn_bcast_ev(eng):
            eng.tensor_copy(wbc[:], pS[:, 0:16])
            eng.tensor_copy(eubc[:], pS[:, 16:32])
            return eng.tensor_copy(nrtc[:], pS[:, 32:33])
        dv(fn_bcast_ev, rd=("pS",), wr=("wbc", "eubc", "nrtc"))

        for h0 in range(0, H, 4):
            def fn_tri_a(eng, h0=h0):
                last = None
                for k in range(4):
                    last = eng.activation(out=rbuf[:, 384 + k * 128:512 + k * 128], in_=sc1[:, 0:128],
                                          func=AF.Exp, scale=wbc[:, h0 + k:h0 + k + 1])
                return last
            ac(fn_tri_a, rd=("sc1", "wbc"), wr=("rbuf",))

            def fn_tri_b(eng, h0=h0):
                last = None
                for k in range(4):
                    last = eng.tensor_mul(tri16[:, h0 + k, :], rbuf[:, 384 + k * 128:512 + k * 128],
                                          ek[:, 0:128])
                return last
            dv(fn_tri_b, rd=("rbuf", "ek"), wr=("tri16",))

        def fn_tri_c(eng):
            last = None
            for hh in range(H):
                last = eng.scalar_tensor_tensor(out=tri16[:, hh, :], in0=rbuf[:, 256:384],
                                                scalar=eubc[:, hh:hh + 1], in1=tri16[:, hh, :],
                                                op0=ALU.mult, op1=ALU.add)
            return last
        dv(fn_tri_c, rd=("rbuf", "eubc", "tri16"), wr=("tri16",))

        def fn_vecs(eng):
            for hh in range(H):
                eng.activation(out=dvec16[0:1, hh * 128:(hh + 1) * 128], in_=rbuf[0:1, 0:128],
                               func=AF.Exp, scale=wrow[0:1, hh:hh + 1])
                eng.activation(out=rvec16[:, hh:hh + 1], in_=rbuf[:, 128:129],
                               func=AF.Exp, scale=wbc[:, hh:hh + 1])
            return eng.activation(out=eL16[:], in_=wrow[:], func=AF.Exp, scale=128.0)
        ac(fn_vecs, rd=("rbuf", "wrow", "wbc"), wr=("dvec16", "rvec16", "eL16"))

        for tl in range(NTILE):
            mWk, mWv, mWr, mWo = 28 * tl, 28 * tl + 1, 28 * tl + 2, 28 * tl + 3

            def emit_C1(cc):
                a0 = cc * 128

                def fn_yt(eng, cc=cc):
                    last = None
                    for j in range(8):
                        last = eng.transpose(pT[:, j * 128:(j + 1) * 128],
                                             yTM[:, cc, j * 128:(j + 1) * 128], idn16[:])
                    return last
                pe(fn_yt, rd=(f"yTM{cc}", "idn16"), wr=("pT",))

                def fn_yev(eng, a0=a0):
                    return eng.activation(out=yFM[:, :, a0:a0 + 128],
                                          in_=pT[:].rearrange("p (c n) -> p c n", n=128),
                                          func=AF.Copy)
                ac(fn_yev, rd=("pT",), wr=(YFM[cc],))

            def emit_A0(cc):
                a0 = cc * 128

                def fn_mix(eng, a0=a0, cc=cc):
                    x3 = xfmt[:, :, a0:a0 + 128]
                    if cc > 0:
                        eng.tensor_sub(xr16[:], x3, xfmt[:, :, a0 - 1:a0 + 127])
                    else:
                        eng.tensor_sub(xr16[:, :, 1:128], xfmt[:, :, 1:128], xfmt[:, :, 0:127])
                        eng.tensor_sub(xr16[:, :, 0:1], xfmt[:, :, 0:1], xlast[:])
                    eng.drain()
                    for jc in range(8):
                        eng.tensor_scalar(out=xk16[:, jc, :], in0=xr16[:, jc, :],
                                          scalar1=mk1c[:, jc:jc + 1], scalar2=None, op0=ALU.mult)
                        eng.tensor_scalar(out=xv16[:, jc, :], in0=xr16[:, jc, :],
                                          scalar1=mv1c[:, jc:jc + 1], scalar2=None, op0=ALU.mult)
                    eng.drain()
                    eng.tensor_add(xk16[:], xk16[:], x3)
                    eng.tensor_add(xv16[:], xv16[:], x3)
                    eng.drain()
                    for jc in range(8):
                        eng.tensor_scalar(out=xr16[:, jc, :], in0=xr16[:, jc, :],
                                          scalar1=mr1c[:, jc:jc + 1], scalar2=None, op0=ALU.mult)
                    eng.drain()
                    return eng.tensor_add(xr16[:], xr16[:], x3)
                if cc <= 1:
                    deps = ARAA
                elif cc == 2:
                    deps = ARAA + ARAB
                elif cc == 3:
                    deps = ARAB
                elif cc == 4:
                    deps = ARAB + ARBH
                else:
                    deps = ARBH
                dv(fn_mix, rd=deps + ("xlast", "mk1c", "mv1c", "mr1c"), wr=("xk16", "xv16", "xr16"))

            def emit_A1(cc, tl=tl, mWk=mWk, mWv=mWv):
                def fn_kproj(eng, m=mWk):
                    w = wslotap(m)
                    last = None
                    for jc in range(8):
                        st, sp = jc == 0, jc == 7
                        eng.matmul(pA[:, 0:512], xk16[:, jc, :], w[:, jc, 0:512], start=st, stop=sp)
                        last = eng.matmul(pA[:, 512:1024], xk16[:, jc, :], w[:, jc, 512:1024], start=st, stop=sp)
                    return last
                pe(fn_kproj, rd=(f"slot{mWk % NSLOT}", "xk16"), wr=("pA0", "pA1"))
                if cc == NCH - 1 and mWk + NSLOT < NW:
                    emit_wdma(mWk + NSLOT)

                def fn_ek(eng):
                    return eng.activation(out=ek[:], in_=pA[:], func=AF.Exp)
                ac(fn_ek, rd=("pA0", "pA1"), wr=("ek",))

                def fn_vproj(eng, m=mWv):
                    w = wslotap(m)
                    last = None
                    for jc in range(8):
                        st, sp = jc == 0, jc == 7
                        eng.matmul(pB[:, 0:512], xv16[:, jc, :], w[:, jc, 0:512], start=st, stop=sp)
                        last = eng.matmul(pB[:, 512:1024], xv16[:, jc, :], w[:, jc, 512:1024], start=st, stop=sp)
                    return last
                pe(fn_vproj, rd=(f"slot{mWv % NSLOT}", "xv16"), wr=("pB0", "pB1"))
                if cc == NCH - 1 and mWv + NSLOT < NW:
                    emit_wdma(mWv + NSLOT)

            def emit_C2(cc):
                a0 = cc * 128

                def fn_route(eng, a0=a0):
                    last = None
                    for dc in range(8):
                        last = eng.matmul(pS[:, 16:24], yFM[:, dc, a0:a0 + 128], wrf16[:, dc, :],
                                          start=(dc == 0), stop=(dc == 7))
                    return last
                pe(fn_route, rd=(YFM[cc], "wrf16"), wr=("pS",))

            def emit_C2b(cc):
                def fn_zev(eng, cc=cc):
                    return eng.activation(out=zroute[:, cc, :], in_=pS[:, 16:24], func=AF.Copy)
                ac(fn_zev, rd=("pS",), wr=("zroute",))

            def emit_B1a(cc, tl=tl):
                def fn_wkv(eng, q=cc % 2):
                    last = None
                    for hh in range(H):
                        bank, b0 = hreg(hh)
                        eng.matmul(bank[:, b0:b0 + 65], tri16[:, hh, :], rhs65[:, q, hh, :],
                                   start=True, stop=False)
                        last = eng.matmul(bank[:, b0:b0 + 65], dvec16[0:1, hh * 128:(hh + 1) * 128],
                                          S16[0:1, hh * 65:(hh + 1) * 65], start=False, stop=True)
                    return last
                pe(fn_wkv, rd=("tri16", "dvec16", f"rhs65{cc % 2}", "S16"), wr=("pN0", "pN1", "pS"))

            def emit_B1b(cc):
                def fn_den2(eng, q=cc % 2):
                    eng.tensor_mul(rr[0][:, 0:16], eks[:, q, :], eubc[:])
                    eng.drain()
                    eng.tensor_scalar(out=rr[1][:, 0:16], in0=rr[0][:, 0:16], scalar1=1e-6,
                                      scalar2=None, op0=ALU.max)
                    eng.drain()
                    eng.tensor_sub(rr[1][:, 0:16], rr[1][:, 0:16], rr[0][:, 0:16])
                    eng.drain()
                    dA = pN[:, 0:455].rearrange("p (h c) -> p h c", c=65)[:, :, 64:65]
                    dB = pN[:, 512:967].rearrange("p (h c) -> p h c", c=65)[:, :, 64:65]
                    dC = pS[:, 64:194].rearrange("p (h c) -> p h c", c=65)[:, :, 64:65]
                    eng.tensor_add(rr[1][:, 0:7].unsqueeze(2), rr[1][:, 0:7].unsqueeze(2), dA)
                    eng.tensor_add(rr[1][:, 7:14].unsqueeze(2), rr[1][:, 7:14].unsqueeze(2), dB)
                    eng.tensor_add(rr[1][:, 14:16].unsqueeze(2), rr[1][:, 14:16].unsqueeze(2), dC)
                    eng.drain()
                    return eng.reciprocal(rr[2][:, 0:16], rr[1][:, 0:16])
                dv(fn_den2, rd=(f"eks{cc % 2}", "eubc", "pN0", "pN1", "pS"), wr=("rr0", "rr1", "rr2"))

                def fn_num(eng):
                    nA = pN[:, 0:455].rearrange("p (h c) -> p h c", c=65)[:, :, 0:64]
                    nB = pN[:, 512:967].rearrange("p (h c) -> p h c", c=65)[:, :, 0:64]
                    nC = pS[:, 64:194].rearrange("p (h c) -> p h c", c=65)[:, :, 0:64]
                    s3 = sc1[:].rearrange("p (h d) -> p h d", d=64)
                    rb = lambda a, b: rr[2][:, a:b].unsqueeze(2).broadcast_to([128, b - a, 64])
                    eng.tensor_mul(s3[:, 0:7], nA, rb(0, 7))
                    eng.tensor_mul(s3[:, 7:14], nB, rb(7, 14))
                    eng.tensor_mul(s3[:, 14:16], nC, rb(14, 16))
                    eng.drain()
                    return eng.tensor_mul(rw16[:], sc1[:], rbuf[:])
                dv(fn_num, rd=("pN0", "pN1", "pS", "rr2", "rbuf"), wr=("sc1", "rw16"))

            def emit_A2(cc, tl=tl, mWr=mWr):
                def fn_rproj(eng, m=mWr):
                    w = wslotap(m)
                    last = None
                    for jc in range(8):
                        st, sp = jc == 0, jc == 7
                        eng.matmul(pA[:, 0:512], xr16[:, jc, :], w[:, jc, 0:512], start=st, stop=sp)
                        last = eng.matmul(pA[:, 512:1024], xr16[:, jc, :], w[:, jc, 512:1024], start=st, stop=sp)
                    return last
                pe(fn_rproj, rd=(f"slot{mWr % NSLOT}", "xr16"), wr=("pA0", "pA1"))
                if cc == NCH - 1 and mWr + NSLOT < NW:
                    emit_wdma(mWr + NSLOT)

            def emit_B2(cc):
                def fn_state(eng, q=cc % 2):
                    last = None
                    for hh in range(H):
                        bank, b0 = hreg(hh)
                        eng.matmul(bank[0:1, b0:b0 + 65], rvec16[:, hh:hh + 1], rhs65[:, q, hh, :],
                                   start=True, stop=False)
                        last = eng.matmul(bank[0:1, b0:b0 + 65], eL16[0:1, hh:hh + 1],
                                          S16[0:1, hh * 65:(hh + 1) * 65], start=False, stop=True)
                    return last
                pe(fn_state, rd=("rvec16", "eL16", f"rhs65{cc % 2}", "S16"), wr=("pN0", "pN1", "pS"))

            def emit_A3(cc):
                def fn_rchain(eng):
                    eng.activation(out=rbuf[:], in_=pA[:], func=AF.Exp, scale=-1.0)
                    eng.drain()
                    eng.activation(out=sc2[:], in_=rbuf[:], func=AF.Ln, bias=onec[:, 0:1])
                    eng.drain()
                    return eng.activation(out=rbuf[:], in_=sc2[:], func=AF.Exp, scale=-1.0)
                ac(fn_rchain, rd=("pA0", "pA1", "onec"), wr=("rbuf", "sc2"))

            def emit_B3(cc):
                def fn_rwt(eng):
                    last = None
                    for j in range(8):
                        last = eng.transpose(pT[:, j * 128:(j + 1) * 128], rw16[:, j * 128:(j + 1) * 128],
                                             idn16[:])
                    return last
                pe(fn_rwt, rd=("rw16", "idn16"), wr=("pT",))

                def fn_rwev(eng):
                    return eng.activation(out=rw16[:].rearrange("p (c n) -> p c n", n=128),
                                          in_=pT[:].rearrange("p (c n) -> p c n", n=128),
                                          func=AF.Copy)
                ac(fn_rwev, rd=("pT", "rw16"), wr=("rw16",))

            def emit_B3b(cc, tl=tl, mWo=mWo):
                def fn_wo(eng, m=mWo):
                    w = wslotap(m)
                    last = None
                    for jc in range(8):
                        st, sp = jc == 0, jc == 7
                        eng.matmul(pB[:, 0:512], rw16[:, jc * 128:(jc + 1) * 128], w[:, jc, 0:512],
                                   start=st, stop=sp)
                        last = eng.matmul(pB[:, 512:1024], rw16[:, jc * 128:(jc + 1) * 128],
                                          w[:, jc, 512:1024], start=st, stop=sp)
                    return last
                pe(fn_wo, rd=(f"slot{mWo % NSLOT}", "rw16"), wr=("pB0", "pB1"))
                if cc == NCH - 1 and mWo + NSLOT < NW:
                    emit_wdma(mWo + NSLOT)

            def emit_B4a(cc):
                def fn_sevac(eng):
                    eng.activation(out=S16[0:1, 0:455], in_=pN[0:1, 0:455], func=AF.Copy)
                    eng.activation(out=S16[0:1, 455:910], in_=pN[0:1, 512:967], func=AF.Copy)
                    return eng.activation(out=S16[0:1, 910:1040], in_=pS[0:1, 64:194], func=AF.Copy)
                ac(fn_sevac, rd=("pN0", "pN1", "pS"), wr=("S16",))

            def emit_B4(cc, tl=tl):
                g = tl * NCH + cc

                def fn_xres(eng, g=g):
                    return eng.tensor_add(sc1[:], xtmb[:, g % 2, :], pB[:])
                dv(fn_xres, rd=(f"xtmb{g % 2}", "pB0", "pB1"), wr=("sc1",))
                if g + 2 < T // 128:
                    def fn_xtm2(eng, g2=g + 2):
                        return eng.dma_start(out=xtmb[:, g2 % 2, :], in_=d_xtm[g2 * 128:(g2 + 1) * 128, :])
                    S.emit("sync", fn_xtm2, rd=(), wr=(f"xtmb{g % 2}",), dma_sem=f"dT{g % 2}")

                def fn_lnC(eng, cc=cc):
                    eng.activation(out=sc2[:], in_=sc1[:], func=AF.Identity, accum_out=msum[:])
                    eng.drain()
                    eng.activation(out=negm[:], in_=msum[:], func=AF.Copy, scale=-1.0 / 1024)
                    eng.drain()
                    eng.activation(out=sc2[:], in_=sc1[:], func=AF.Square, bias=negm[:, 0:1],
                                   accum_out=vsum[:])
                    eng.drain()
                    eng.activation(out=tln[:], in_=vsum[:], func=AF.Ln, scale=1.0 / 1024, bias=epsc[:, 0:1])
                    eng.activation(out=rstd[:], in_=tln[:], func=AF.Exp, scale=-0.5)
                    eng.drain()
                    eng.activation(out=nmr[:], in_=negm[:], func=AF.Identity, scale=rstd[:, 0:1])
                    eng.drain()
                    return eng.activation(out=yTM[:, cc, :], in_=sc1[:], func=AF.Identity,
                                          scale=rstd[:, 0:1], bias=nmr[:, 0:1])
                ac(fn_lnC, rd=("sc1", "epsc"), wr=("sc2", "msum", "negm", "vsum", "tln", "rstd",
                                                   "nmr", f"yTM{cc}"))

            def emit_A4(cc):
                def fn_ekv(eng, q=cc % 2):
                    v3 = lambda t: t[:].rearrange("p (h d) -> p h d", d=64)
                    eng.tensor_mul(rhs65[:, q, :, 0:64], v3(ek), v3(pB))
                    eng.drain()
                    eng.reduce_sum(eks[:, q, :], v3(ek), axis=AX.X)
                    eng.drain()
                    return eng.tensor_copy(rhs65[:, q, :, 64:65], eks[:, q, :].unsqueeze(2))
                dv(fn_ekv, rd=("ek", "pB0", "pB1"), wr=(f"rhs65{cc % 2}", f"eks{cc % 2}"))

            if tl == 0:
                emit_A0(0)
                emit_A1(0)
                emit_A4(0)
            for it in range(NCH + 2):
                if 0 < it < NCH:
                    emit_A1(it)
                    emit_A4(it)
                if 1 <= it <= NCH:
                    emit_B1a(it - 1)
                    if it == 1 and tl > 0:
                        emit_A2(0)
                        emit_A3(0)
                    emit_B1b(it - 1)
                if it < NCH and not (it == 0 and tl > 0):
                    emit_A2(it)
                if 1 <= it <= NCH:
                    emit_B3(it - 1)
                if it < NCH and not (it == 0 and tl > 0):
                    emit_A3(it)
                if 1 <= it <= NCH:
                    emit_B2(it - 1)
                    emit_B4a(it - 1)
                if it + 1 < NCH and not (it == 0 and tl > 0):
                    emit_A0(it + 1)
                if 2 <= it:
                    emit_C1(it - 2)
                if 1 <= it <= NCH:
                    emit_B3b(it - 1)
                if 2 <= it:
                    emit_C2(it - 2)
                if 1 <= it <= NCH:
                    emit_B4(it - 1)
                if 2 <= it:
                    emit_C2b(it - 2)
            def fn_xl(eng):
                return eng.tensor_copy(xlast[:], xfmt[:, :, 1023:1024])
            dv(fn_xl, rd=("arB7",), wr=("xlast",))
            emit_A0_last = emit_A0
            emit_A1_last = emit_A1
            emit_A4_last = emit_A4

            def fn_rt_act(eng):
                zv = zroute[:].rearrange("p c e -> p (c e)")
                eng.activation(out=rr[0][:], in_=zv, func=AF.Exp, scale=nrtc[:, 0:1])
                eng.activation(out=rr[1][:], in_=rr[0][:], func=AF.Ln, bias=onec[:, 0:1])
                eng.activation(out=rr[0][:], in_=rr[1][:], func=AF.Exp, scale=-1.0)
                eng.activation(out=rr[1][:], in_=rr[0][:], func=AF.Abs, bias=ngcc[:, 0:1])
                eng.activation(out=rr[2][:], in_=rr[1][:], func=AF.Exp, scale=-10.0)
                return eng.activation(out=rr[3][:], in_=rr[1][:], func=AF.Exp, scale=-1.0 / 0.3)
            ac(fn_rt_act, rd=("zroute", "nrtc", "onec", "ngcc"), wr=("rr0", "rr1", "rr2", "rr3"))

            def fn_rt_dve(eng):
                v88 = lambda t: t[:].rearrange("p (c e) -> p c e", e=8)
                b8 = lambda t: t[:].unsqueeze(2).broadcast_to([128, 8, 8])
                eng.tensor_scalar(out=rr[4][:], in0=rr[0][:], scalar1=GL, scalar2=None, op0=ALU.is_ge)
                eng.drain()
                eng.tensor_scalar(out=rr[1][:], in0=rr[0][:], scalar1=GU, scalar2=None, op0=ALU.is_le)
                eng.drain()
                eng.tensor_mul(rr[4][:], rr[4][:], rr[1][:])
                eng.drain()
                eng.tensor_mul(rr[2][:], rr[2][:], rr[4][:])
                eng.drain()
                eng.reduce_sum(rs8a[:], v88(rr[2]), axis=AX.X)
                eng.drain()
                eng.tensor_scalar(out=rne[:], in0=rs8a[:], scalar1=1e-8, scalar2=None, op0=ALU.is_lt)
                eng.drain()
                fb = rr[3][:].rearrange("p (c e) -> p c e", e=8)
                e4a = fb.unsqueeze(2).broadcast_to([128, 8, 8, 8])
                e4b = fb.unsqueeze(3).broadcast_to([128, 8, 8, 8])
                cnt4 = sc2[:, 0:512].rearrange("p (c e f) -> p c e f", e=8, f=8)
                eng.tensor_tensor(cnt4, e4a, e4b, ALU.is_gt)
                eng.drain()
                eng.reduce_sum(v88(rr[4]), cnt4, axis=AX.X)
                eng.drain()
                eng.tensor_scalar(out=rr[4][:], in0=rr[4][:], scalar1=1.5, scalar2=None, op0=ALU.is_le)
                eng.drain()
                eng.tensor_mul(rr[4][:], rr[3][:], rr[4][:])
                eng.drain()
                eng.reduce_sum(rs8b[:], v88(rr[4]), axis=AX.X)
                eng.drain()
                eng.tensor_scalar(out=rs8b[:], in0=rs8b[:], scalar1=1e-8, scalar2=None, op0=ALU.max)
                eng.drain()
                eng.reciprocal(rs8a[:], rs8b[:])
                eng.drain()
                eng.tensor_mul(v88(rr[4]), v88(rr[4]), b8(rs8a))
                eng.drain()
                eng.tensor_sub(rr[4][:], rr[4][:], rr[2][:])
                eng.drain()
                eng.tensor_mul(v88(rr[4]), v88(rr[4]), b8(rne))
                eng.drain()
                eng.tensor_add(rr[2][:], rr[2][:], rr[4][:])
                eng.drain()
                eng.reduce_sum(rs8b[:], v88(rr[2]), axis=AX.X)
                eng.drain()
                eng.tensor_scalar(out=rs8b[:], in0=rs8b[:], scalar1=1e-8, scalar2=None, op0=ALU.max)
                eng.drain()
                eng.reciprocal(rs8a[:], rs8b[:])
                eng.drain()
                return eng.tensor_mul(wts[:], v88(rr[2]), b8(rs8a))
            dv(fn_rt_dve, rd=("rr0", "rr1", "rr2", "rr3"),
               wr=("rr1", "rr2", "rr4", "sc2", "rs8a", "rs8b", "rne", "wts"))

            def emit_mixb(jc):
                def fn_mixb(eng, jc=jc):
                    x3 = xfmt[:, jc, 0:128]
                    eng.tensor_sub(xr16[:, jc, 1:128], xfmt[:, jc, 1:128],
                                   xfmt[:, jc, 0:127])
                    eng.tensor_sub(xr16[:, jc, 0:1], xfmt[:, jc, 0:1], xlast[:, jc, :])
                    eng.drain()
                    eng.tensor_scalar(out=xk16[:, jc, :], in0=xr16[:, jc, :],
                                      scalar1=mk1c[:, jc:jc + 1], scalar2=None, op0=ALU.mult)
                    eng.tensor_scalar(out=xv16[:, jc, :], in0=xr16[:, jc, :],
                                      scalar1=mv1c[:, jc:jc + 1], scalar2=None, op0=ALU.mult)
                    eng.drain()
                    eng.tensor_add(xk16[:, jc, :], xk16[:, jc, :], x3)
                    eng.tensor_add(xv16[:, jc, :], xv16[:, jc, :], x3)
                    eng.drain()
                    eng.tensor_scalar(out=xr16[:, jc, :], in0=xr16[:, jc, :],
                                      scalar1=mr1c[:, jc:jc + 1], scalar2=None, op0=ALU.mult)
                    eng.drain()
                    return eng.tensor_add(xr16[:, jc, :], xr16[:, jc, :], x3)
                dv(fn_mixb, rd=(f"arAA{jc}", "xlast", "mk1c", "mv1c", "mr1c"),
                   wr=("xk16", "xv16", "xr16"))

            def emit_stageE(cc, tl=tl):
                g = tl * NCH + cc
                xb, scr, ng, rs = (sc1, sc2, negm, rstd) if cc % 2 == 0 else (rbuf, ek, negmB, rstdB)
                xbr, scrr = ("sc1", "sc2") if cc % 2 == 0 else ("rbuf", "ek")
                ngr, rsr = ("negm", "rstd") if cc % 2 == 0 else ("negmB", "rstdB")

                def fn_xres2(eng, cc=cc, xb=xb):
                    return eng.tensor_add(xb[:], yTM[:, cc, :], accv[:, cc, :])
                dv(fn_xres2, rd=(f"yTM{cc}", f"arAA{cc}", f"arAB{cc}", f"arB{cc}"), wr=(xbr,))

                def fn_lnE(eng, xb=xb, scr=scr, ng=ng, rs=rs):
                    eng.activation(out=scr[:], in_=xb[:], func=AF.Identity, accum_out=msum[:])
                    eng.drain()
                    eng.activation(out=ng[:], in_=msum[:], func=AF.Copy, scale=-1.0 / 1024)
                    eng.drain()
                    eng.activation(out=scr[:], in_=xb[:], func=AF.Square, bias=ng[:, 0:1],
                                   accum_out=vsum[:])
                    eng.drain()
                    eng.activation(out=tln[:], in_=vsum[:], func=AF.Ln, scale=1.0 / 1024, bias=epsc[:, 0:1])
                    return eng.activation(out=rs[:], in_=tln[:], func=AF.Exp, scale=-0.5)
                ac(fn_lnE, rd=(xbr, "epsc"), wr=(scrr, "msum", ngr, "vsum", "tln", rsr))

                def fn_out(eng, xb=xb, ng=ng, rs=rs):
                    eng.activation(out=nmr[:], in_=ng[:], func=AF.Identity, scale=rs[:, 0:1])
                    eng.drain()
                    return eng.activation(out=outb[:], in_=xb[:], func=AF.Identity,
                                          scale=rs[:, 0:1], bias=nmr[:, 0:1])
                ac(fn_out, rd=(xbr, ngr, rsr), wr=("outb", "nmr"))

                def fn_odma(eng, g=g):
                    return eng.dma_start(out=d_out[g * 128:(g + 1) * 128, :], in_=outb[:])
                S.emit("sync" if cc < 4 else "scalar", fn_odma, rd=("outb",), wr=(), dma_sem="dO")

            def emit_tail(cc, tl=tl):
                emit_stageE(cc)
                if tl + 1 < NTILE:
                    t0n = (tl + 1) * TTILE
                    if cc in (1, 3):
                        def fn_pairA(eng, j0=cc - 1, t0=t0n):
                            return eng.dma_start(
                                out=xfmt[:, j0:j0 + 2, 0:512],
                                in_=d_xfm[j0 * 128:(j0 + 2) * 128, t0:t0 + 512]
                                .rearrange("(c p) n -> p c n", p=128))
                        S.emit("scalar", fn_pairA, rd=(),
                               wr=(f"arAA{cc - 1}", f"arAB{cc - 1}", f"arAA{cc}", f"arAB{cc}"),
                               dma_sem=f"dX{cc - 1}")
                    if cc == 3:
                        def fn_bulk03(eng, t0=t0n):
                            return eng.dma_start(
                                out=xfmt[:, 0:4, 512:1024],
                                in_=d_xfm[0:512, t0 + 512:t0 + 1024]
                                .rearrange("(c p) n -> p c n", p=128))
                        S.emit("sync", fn_bulk03, rd=(),
                               wr=tuple(f"arB{j}" for j in range(4)), dma_sem="dXB0")
                    if cc in (5, 7):
                        def fn_pairB(eng, j0=cc - 1, t0=t0n):
                            return eng.dma_start(
                                out=xfmt[:, j0:j0 + 2, 0:256],
                                in_=d_xfm[j0 * 128:(j0 + 2) * 128, t0:t0 + 256]
                                .rearrange("(c p) n -> p c n", p=128))
                        S.emit("sync", fn_pairB, rd=(),
                               wr=(f"arAA{cc - 1}", f"arAA{cc}"), dma_sem=f"dXA{cc - 1}")
                    if cc >= 1:
                        emit_mixb(cc - 1)

            def emit_tail2(tl=tl):
                if tl + 1 >= NTILE:
                    return
                t0n = (tl + 1) * TTILE

                def fn_bulk47(eng, t0=t0n):
                    return eng.dma_start(
                        out=xfmt[:, 4:8, 256:1024],
                        in_=d_xfm[512:1024, t0 + 256:t0 + 1024]
                        .rearrange("(c p) n -> p c n", p=128))
                S.emit("sync", fn_bulk47, rd=(),
                       wr=tuple(f"arAB{j}" for j in range(4, 8))
                       + tuple(f"arB{j}" for j in range(4, 8)), dma_sem="dXB4")

            for e in range(E):
                mG, mU, mD = 28 * tl + 4 + 3 * e, 28 * tl + 5 + 3 * e, 28 * tl + 6 + 3 * e
                for half in range(2):
                    t0 = half * 512
                    ycols = tuple(YFM[half * 4 + i] for i in range(4))
                    for ic in range(8):
                        p = ic % 2
                        gb = pA[:, 0:512] if p == 0 else pA[:, 512:1024]
                        ub = pN[:, 0:512] if p == 0 else pN[:, 512:1024]

                        def fn_gate(eng, m=mG, ic=ic, t0=t0, gb=gb):
                            w = wslotap(m)
                            last = None
                            for dc in range(8):
                                last = eng.matmul(gb, w[:, dc, ic * 128:(ic + 1) * 128],
                                                  yFM[:, dc, t0:t0 + 512],
                                                  start=(dc == 0), stop=(dc == 7))
                            return last
                        pe(fn_gate, rd=(f"slot{mG % NSLOT}",) + ycols, wr=(f"pA{p}",))
                        if half == 1 and ic == 7 and mG + NSLOT < NW:
                            emit_wdma(mG + NSLOT)

                        def fn_silu(eng, p=p, gb=gb):
                            return eng.activation(out=g_s[:, p, :], in_=gb, func=AF.Silu)
                        ac(fn_silu, rd=(f"pA{p}",), wr=(f"g_s{p}",))

                        def fn_up(eng, m=mU, ic=ic, t0=t0, ub=ub):
                            w = wslotap(m)
                            last = None
                            for dc in range(8):
                                last = eng.matmul(ub, w[:, dc, ic * 128:(ic + 1) * 128],
                                                  yFM[:, dc, t0:t0 + 512],
                                                  start=(dc == 0), stop=(dc == 7))
                            return last
                        pe(fn_up, rd=(f"slot{mU % NSLOT}",) + ycols, wr=(f"pN{p}",))
                        if half == 1 and ic == 7 and mU + NSLOT < NW:
                            emit_wdma(mU + NSLOT)

                        def fn_h(eng, ic=ic, p=p, ub=ub):
                            return eng.tensor_mul(h16[:, ic, :], g_s[:, p, :], ub)
                        dv(fn_h, rd=(f"g_s{p}", f"pN{p}"), wr=(f"h{ic}",))

                    for tc in range(4):
                        cc_g = half * 4 + tc
                        bank, brs = (pA, ("pA0", "pA1")) if tc % 2 == 0 else (pB, ("pB0", "pB1"))

                        def fn_down_a(eng, m=mD, tc=tc, bank=bank):
                            w = wslotap(m)
                            last = None
                            for ic in range(7):
                                st = ic == 0
                                eng.matmul(bank[:, 0:512], h16[:, ic, tc * 128:(tc + 1) * 128],
                                           w[:, ic, 0:512], start=st, stop=False)
                                last = eng.matmul(bank[:, 512:1024], h16[:, ic, tc * 128:(tc + 1) * 128],
                                                  w[:, ic, 512:1024], start=st, stop=False)
                            return last
                        pe(fn_down_a, rd=(f"slot{mD % NSLOT}",) + tuple(f"h{i}" for i in range(7)),
                           wr=brs)

                        def fn_down_b(eng, m=mD, tc=tc, bank=bank):
                            w = wslotap(m)
                            eng.matmul(bank[:, 0:512], h16[:, 7, tc * 128:(tc + 1) * 128],
                                       w[:, 7, 0:512], start=False, stop=True)
                            return eng.matmul(bank[:, 512:1024], h16[:, 7, tc * 128:(tc + 1) * 128],
                                              w[:, 7, 512:1024], start=False, stop=True)
                        pe(fn_down_b, rd=(f"slot{mD % NSLOT}", "h7"), wr=brs)
                        if half == 1 and tc == 3 and mD + NSLOT < NW:
                            emit_wdma(mD + NSLOT)

                        def fn_acc(eng, cc_g=cc_g, e=e, bank=bank):
                            if e == 0:
                                return eng.tensor_scalar(out=accv[:, cc_g, :], in0=bank[:],
                                                         scalar1=wts[:, cc_g, e:e + 1], scalar2=None,
                                                         op0=ALU.mult)
                            return eng.scalar_tensor_tensor(out=accv[:, cc_g, :], in0=bank[:],
                                                            scalar=wts[:, cc_g, e:e + 1],
                                                            in1=accv[:, cc_g, :],
                                                            op0=ALU.mult, op1=ALU.add)
                        dv(fn_acc, rd=brs + ("wts", f"arAA{cc_g}", f"arAB{cc_g}", f"arB{cc_g}"),
                           wr=(f"arAA{cc_g}", f"arAB{cc_g}", f"arB{cc_g}"))

                        if e == E - 1 and cc_g >= 1:
                            emit_tail(cc_g - 1)
            emit_tail(7)
            emit_tail2()
            if tl + 1 < NTILE:
                emit_mixb(7)
                emit_A1_last(0, mWk=28 * (tl + 1), mWv=28 * (tl + 1) + 1)
                emit_A0_last(1)
                emit_A4_last(0)


        def fn_fin(eng):
            return None
        S.emit("sync", fn_fin, rd=(), wr=(), extra=[("dO", 16 * (T // 128))], bare=True)

        with nc.Block() as block:
            def mk(eng_name):
                def body(engine):
                    waited = defaultdict(int)
                    for (en, fn, waits, inc, n) in S.steps:
                        if en != eng_name:
                            continue
                        for (sname, v) in waits:
                            if v > waited[sname]:
                                engine.wait_ge(sems[sname], v)
                                waited[sname] = v
                        last = fn(engine)
                        if inc is not None and last is not None:
                            last.then_inc(sems[inc], n)
                        elif inc is not None:
                            raise RuntimeError(f"step on {eng_name} returned no instruction")
                return body
            block.sync(mk("sync"))
            block.gpsimd(mk("gpsimd"))
            block.vector(mk("vector"))
            block.scalar(mk("scalar"))
            block.tensor(mk("tensor"))

    return nc


def kernel(**inp):
    f = np.float32
    x = np.asarray(inp["L0_attn_x"], f)
    wkT = np.ascontiguousarray(np.asarray(inp["L0_attn_Wk"], f).T)
    wvT = np.ascontiguousarray(np.asarray(inp["L0_attn_Wv"], f).T)
    wrT = np.ascontiguousarray(np.asarray(inp["L0_attn_Wr"], f).T)
    woT = np.ascontiguousarray(np.asarray(inp["L0_attn_Wo"], f).T)
    wg = np.asarray(inp["L0_ffn_Wg"], f)
    wu = np.asarray(inp["L0_ffn_Wu"], f)
    wd = np.asarray(inp["L0_ffn_Wd"], f)
    wrf = np.ascontiguousarray(np.asarray(inp["L0_ffn_Wr"], f).reshape(8, 128, 8).transpose(1, 0, 2))
    mk_ = np.ascontiguousarray(np.asarray(inp["L0_attn_mix_k"], f).reshape(8, 128).T)
    mv_ = np.ascontiguousarray(np.asarray(inp["L0_attn_mix_v"], f).reshape(8, 128).T)
    mr_ = np.ascontiguousarray(np.asarray(inp["L0_attn_mix_r"], f).reshape(8, 128).T)
    td = np.asarray(inp["L0_attn_time_decay"], f).reshape(1, 16)
    tf = np.asarray(inp["L0_attn_time_first"], f).reshape(1, 16)
    temp = np.asarray(inp["L0_ffn_temperature"], f).reshape(1, 1)

    s = np.arange(128, dtype=f)
    dmat = (s[None, :] - 1.0 - s[:, None]).astype(f)
    dmask = (dmat >= 0).astype(f)
    ior = s.reshape(1, 128)
    ioc = (127.0 - s).reshape(128, 1)
    idn = np.eye(128, dtype=f)

    nc = build_nc()
    in_maps = []
    for b in range(NCORE):
        m = {"xfm": np.ascontiguousarray(x[b].T), "xtm": np.ascontiguousarray(x[b]),
             "wkT": wkT, "wvT": wvT, "wrT": wrT, "woT": woT,
             "wrf": wrf, "mk": mk_, "mv": mv_, "mr": mr_,
             "td": td, "tf": tf, "temp": temp,
             "dmat": dmat, "dmask": dmask, "ior": ior, "ioc": ioc, "idn": idn}
        for e in range(E):
            m[f"wg{e}"] = wg[e]
            m[f"wu{e}"] = wu[e]
            m[f"wd{e}"] = wd[e]
        in_maps.append(m)

    res = run_bass_kernel_spmd(nc, in_maps, core_ids=list(range(NCORE)))
    return np.stack([np.asarray(r["out"]) for r in res.results], axis=0)


# version 29
# speedup vs baseline: 1.0247x; 1.0247x over previous
import numpy as np
import contextlib
from collections import defaultdict

import concourse.bass as bass
import concourse.mybir as mybir
from concourse.bass_utils import run_bass_kernel_spmd

F32 = mybir.dt.float32
BF16 = mybir.dt.bfloat16
AF = mybir.ActivationFunctionType
ALU = mybir.AluOpType
AX = mybir.AxisListType

T, C, H, HD, E = 4096, 1024, 16, 64, 8
NCORE = 8
TTILE = 1024
NTILE = T // TTILE
NCH = TTILE // 128
NSLOT = 5
EPS = 1e-5
GC = 0.36787944117144233
GL = 0.21231792754821915
GU = 0.5

ESEM = {"vector": "sDVE", "scalar": "sACT", "tensor": "sPE", "gpsimd": "sGP"}


class Sched:

    def __init__(self):
        self.steps = []
        self.cnt = defaultdict(int)
        self.last_w = {}
        self.readers = defaultdict(dict)

    def _merge(self, waits, tok):
        sem, val = tok
        if val > waits.get(sem, 0):
            waits[sem] = val

    def emit(self, eng, fn, rd=(), wr=(), extra=(), dma_sem=None, self_sync=True, bare=False):
        waits = {}
        for (s, v) in extra:
            self._merge(waits, (s, v))
        for r in rd:
            w = self.last_w.get(r)
            if w is not None and w[0] != eng:
                self._merge(waits, (w[1], w[2]))
        for r in wr:
            w = self.last_w.get(r)
            if w is not None and w[0] != eng:
                self._merge(waits, (w[1], w[2]))
            for k, tok in self.readers.get(r, {}).items():
                if k != eng:
                    self._merge(waits, tok)
        if bare:
            self.steps.append((eng, fn, sorted(waits.items()), None, 0))
            return None
        if dma_sem is None:
            sem = ESEM[eng]
            if self_sync and eng in ("vector", "scalar", "gpsimd") and self.cnt[sem] > 0:
                need = False
                for r in rd:
                    w = self.last_w.get(r)
                    if w is not None and w[0] == eng:
                        need = True
                for r in wr:
                    w = self.last_w.get(r)
                    if w is not None and w[0] == eng:
                        need = True
                    if eng in self.readers.get(r, {}):
                        need = True
                if need:
                    self._merge(waits, (sem, self.cnt[sem]))
            self.cnt[sem] += 1
            tok_after = (sem, self.cnt[sem])
            inc, inc_n = sem, 1
            key = eng
        else:
            self.cnt[dma_sem] += 16
            tok_after = (dma_sem, self.cnt[dma_sem])
            inc, inc_n = dma_sem, 16
            key = "#" + dma_sem
        self.steps.append((eng, fn, sorted(waits.items()), inc, inc_n))
        for r in wr:
            self.last_w[r] = (key, tok_after[0], tok_after[1])
            self.readers[r] = {}
        for r in rd:
            self.readers[r][key] = tok_after
        return tok_after

    def tok(self, sem):
        return (sem, self.cnt[sem])


def build_nc():
    nc = bass.Bass("TRN2", target_bir_lowering=False, debug=False)

    d_xfm = nc.dram_tensor("xfm", [C, T], F32, kind="ExternalInput").ap()
    d_xtm = nc.dram_tensor("xtm", [T, C], F32, kind="ExternalInput").ap()
    d_wk = nc.dram_tensor("wkT", [C, C], F32, kind="ExternalInput").ap()
    d_wv = nc.dram_tensor("wvT", [C, C], F32, kind="ExternalInput").ap()
    d_wr = nc.dram_tensor("wrT", [C, C], F32, kind="ExternalInput").ap()
    d_wo = nc.dram_tensor("woT", [C, C], F32, kind="ExternalInput").ap()
    d_wg = [nc.dram_tensor(f"wg{e}", [C, C], F32, kind="ExternalInput").ap() for e in range(E)]
    d_wu = [nc.dram_tensor(f"wu{e}", [C, C], F32, kind="ExternalInput").ap() for e in range(E)]
    d_wd = [nc.dram_tensor(f"wd{e}", [C, C], F32, kind="ExternalInput").ap() for e in range(E)]
    d_wrf = nc.dram_tensor("wrf", [128, 8, 8], F32, kind="ExternalInput").ap()
    d_mk = nc.dram_tensor("mk", [128, 8], F32, kind="ExternalInput").ap()
    d_mv = nc.dram_tensor("mv", [128, 8], F32, kind="ExternalInput").ap()
    d_mr = nc.dram_tensor("mr", [128, 8], F32, kind="ExternalInput").ap()
    d_td = nc.dram_tensor("td", [1, 16], F32, kind="ExternalInput").ap()
    d_tf = nc.dram_tensor("tf", [1, 16], F32, kind="ExternalInput").ap()
    d_temp = nc.dram_tensor("temp", [1, 1], F32, kind="ExternalInput").ap()
    d_dmat = nc.dram_tensor("dmat", [128, 128], F32, kind="ExternalInput").ap()
    d_mask = nc.dram_tensor("dmask", [128, 128], F32, kind="ExternalInput").ap()
    d_ior = nc.dram_tensor("ior", [1, 128], F32, kind="ExternalInput").ap()
    d_ioc = nc.dram_tensor("ioc", [128, 1], F32, kind="ExternalInput").ap()
    d_idn = nc.dram_tensor("idn", [128, 128], F32, kind="ExternalInput").ap()
    d_out = nc.dram_tensor("out", [T, C], F32, kind="ExternalOutput").ap()

    def wsrc(ap2d):
        return ap2d.rearrange("(c p) n -> p c n", p=128)

    stream = []
    for _tl in range(NTILE):
        stream += [wsrc(d_wk), wsrc(d_wv), wsrc(d_wr), wsrc(d_wo)]
        for e in range(E):
            stream += [wsrc(d_wg[e]), wsrc(d_wu[e]), wsrc(d_wd[e])]
    NW = len(stream)

    S = Sched()

    with contextlib.ExitStack() as X:
        def sb(name, shape, dt=F32):
            return X.enter_context(nc.sbuf_tensor(name, shape, dt))

        def ps(name, shape, dt=F32):
            return X.enter_context(nc.psum_tensor(name, shape, dt))

        slots = sb("slots", [128, NSLOT, 8, 1024], BF16)
        arena = sb("arena", [128, 8 * 1024], F32)
        xlast = sb("xlast", [128, 8, 1], F32)
        yTM = sb("yTM", [128, NCH, 1024], BF16)
        yFM = sb("yFM", [128, 8, 1024], BF16)
        h16 = sb("h16", [128, 8, 512], BF16)
        ek = sb("ek", [128, 1024], F32)
        rhs65 = sb("rhs65", [128, 2, 16, 65], BF16)
        eks = sb("eks", [128, 2, 16], F32)
        rbuf = sb("rbuf", [128, 1024], F32)
        sc1 = sb("sc1", [128, 1024], F32)
        rw16 = sb("rw16", [128, 1024], BF16)
        g_s = sb("g_s", [128, 2, 512], BF16)
        xtmb = sb("xtmb", [128, 2, 1024], F32)
        outb = sb("outb", [128, 1024], F32)
        sc2 = sb("sc2", [128, 1024], F32)
        zroute = sb("zroute", [128, NCH, 8], F32)
        wts = sb("wts", [128, NCH, 8], F32)
        rr = [sb(f"rr{i}", [128, 64], F32) for i in range(5)]
        rs8a = sb("rs8a", [128, 8], F32)
        rs8b = sb("rs8b", [128, 8], F32)
        rne = sb("rne", [128, 8], F32)
        msum = sb("msum", [128, 1], F32)
        negm = sb("negm", [128, 1], F32)
        vsum = sb("vsum", [128, 1], F32)
        tln = sb("tln", [128, 1], F32)
        rstd = sb("rstd", [128, 1], F32)
        nmr = sb("nmr", [128, 1], F32)
        negmB = sb("negmB", [128, 1], F32)
        rstdB = sb("rstdB", [128, 1], F32)
        tri16 = sb("tri16", [128, 16, 128], BF16)
        idn16 = sb("idn16", [128, 128], BF16)
        dvec16 = sb("dvec16", [1, 16 * 128], BF16)
        rvec16 = sb("rvec16", [128, 16], BF16)
        eL16 = sb("eL16", [1, 16], BF16)
        wrow = sb("wrow", [1, 16], F32)
        eurow = sb("eurow", [1, 16], F32)
        wbc = sb("wbc", [128, 16], F32)
        eubc = sb("eubc", [128, 16], F32)
        nrtc = sb("nrtc", [128, 1], F32)
        mk1c = sb("mk1c", [128, 8], F32)
        mv1c = sb("mv1c", [128, 8], F32)
        mr1c = sb("mr1c", [128, 8], F32)
        tempr = sb("tempr", [1, 1], F32)
        rtmp1 = sb("rtmp1", [1, 1], F32)
        onesf = sb("onesf", [1, 128], F32)
        onec = sb("onec", [128, 1], F32)
        epsc = sb("epsc", [128, 1], F32)
        ngcc = sb("ngcc", [128, 1], F32)
        S16 = sb("S16", [1, 1024 + 16], BF16)
        wrf16 = sb("wrf16", [128, 8, 8], BF16)
        xk16 = sb("xk16", [128, 8, 128], BF16)
        xv16 = sb("xv16", [128, 8, 128], BF16)
        xr16 = sb("xr16", [128, 8, 128], BF16)

        pA = ps("pA", [128, 1024], F32)
        pB = ps("pB", [128, 1024], F32)
        pN = ps("pN", [128, 1024], F32)
        pS = ps("pS", [128, 512], F32)
        pT = ps("pT", [128, 1024], BF16)

        sems = {n: X.enter_context(nc.semaphore(n)) for n in
                ("sPE", "sDVE", "sACT", "sGP", "dO", "dM", "dF")
                + tuple(f"dW{i}" for i in range(NSLOT))
                + tuple(f"dX{i}" for i in range(8))
                + tuple(f"dXB{i}" for i in range(8))
                + tuple(f"dXA{i}" for i in range(4, 8)) + ("dT0", "dT1")}

        xfmt = arena[:].rearrange("p (c n) -> p c n", n=1024)
        accv = xfmt
        ARAA = tuple(f"arAA{i}" for i in range(8))
        ARAB = tuple(f"arAB{i}" for i in range(8))
        ARBH = tuple(f"arB{i}" for i in range(8))
        ARB = ARAA + ARAB + ARBH

        def wslotap(m):
            return slots[:, m % NSLOT]

        def hreg(hh):
            if hh < 7:
                return pN, hh * 65
            if hh < 14:
                return pN, 512 + (hh - 7) * 65
            return pS, 64 + (hh - 14) * 65

        YFM = [f"yFM{i}" for i in range(8)]

        def pe(fn, rd=(), wr=()):
            S.emit("tensor", fn, rd, wr)

        def dv(fn, rd=(), wr=()):
            S.emit("vector", fn, rd, wr)

        def ac(fn, rd=(), wr=()):
            S.emit("scalar", fn, rd, wr)

        def gp(fn, rd=(), wr=()):
            S.emit("gpsimd", fn, rd, wr)

        def emit_wdma(m):
            s = m % NSLOT
            src = stream[m]

            def fn(eng, s=s, src=src):
                return eng.dma_start(out=slots[:, s], in_=src)
            S.emit("gpsimd", fn, rd=(), wr=(f"slot{s}",), dma_sem=f"dW{s}")

        def sload(dst_ap, src_ap, res):
            def fn(eng):
                return eng.dma_start(out=dst_ap, in_=src_ap)
            S.emit("sync", fn, rd=(), wr=(res,), dma_sem="dM")

        mkc = rbuf[:, 896:904]
        mvc = rbuf[:, 904:912]
        mrc = rbuf[:, 912:920]
        for jc in range(8):
            def fn_xfm0a(eng, jc=jc):
                return eng.dma_start(out=xfmt[:, jc, 0:512],
                                     in_=d_xfm[jc * 128:(jc + 1) * 128, 0:512])
            S.emit("sync", fn_xfm0a, rd=(), wr=(f"arAA{jc}", f"arAB{jc}"), dma_sem=f"dX{jc}")

            def fn_xfm0b(eng, jc=jc):
                return eng.dma_start(out=xfmt[:, jc, 512:1024],
                                     in_=d_xfm[jc * 128:(jc + 1) * 128, 512:1024])
            S.emit("scalar", fn_xfm0b, rd=(), wr=(f"arB{jc}",), dma_sem=f"dXB{jc}")

        sload(mkc, d_mk, "ld0")
        sload(mvc, d_mv, "ld1")
        sload(mrc, d_mr, "ld2")
        tdr = rbuf[0:1, 512:528]
        tfr = rbuf[0:1, 528:544]
        wtmp = rbuf[0:1, 544:560]
        sload(tdr, d_td, "ld3")
        sload(tfr, d_tf, "ld4")
        sload(tempr[:], d_temp, "ld5")
        sload(sc1[:, 0:128], d_dmat, "ld6")
        sload(ek[:, 0:128], d_mask, "ld7")
        sload(rbuf[0:1, 0:128], d_ior, "ld8")
        sload(rbuf[:, 128:129], d_ioc, "ld9")
        sload(rbuf[:, 256:384], d_idn, "ld10")

        _full = S.cnt["dM"]
        k, sem, _v = S.last_w["ld0"]
        for r in ("tempr", "sc1", "ek", "rbuf"):
            S.last_w[r] = (k, sem, _full)

        for g in (0, 1):
            def fn_xtm(eng, g=g):
                return eng.dma_start(out=xtmb[:, g % 2, :], in_=d_xtm[g * 128:(g + 1) * 128, :])
            S.emit("sync", fn_xtm, rd=(), wr=(f"xtmb{g % 2}",), dma_sem=f"dT{g % 2}")

        def fn_wrf(eng):
            return eng.dma_start(out=wrf16[:], in_=d_wrf)
        S.emit("gpsimd", fn_wrf, rd=(), wr=("wrf16",), dma_sem="dF")
        for m in range(NSLOT):
            emit_wdma(m)

        def fn_mix1(eng):
            eng.tensor_scalar(out=mk1c[:], in0=mkc, scalar1=1.0, scalar2=None, op0=ALU.subtract)
            eng.tensor_scalar(out=mv1c[:], in0=mvc, scalar1=1.0, scalar2=None, op0=ALU.subtract)
            return eng.tensor_scalar(out=mr1c[:], in0=mrc, scalar1=1.0, scalar2=None, op0=ALU.subtract)
        dv(fn_mix1, rd=("rbuf",), wr=("mk1c", "mv1c", "mr1c"))

        def fn_memset(eng):
            eng.memset(onesf[:], 1.0)
            eng.memset(onec[:], 1.0)
            eng.memset(epsc[:], EPS)
            eng.memset(ngcc[:], -GC)
            eng.memset(S16[:], 0.0)
            eng.memset(xlast[:], 0.0)
            return eng.tensor_copy(idn16[:], rbuf[:, 256:384])
        dv(fn_memset, rd=("rbuf",), wr=("onesf", "onec", "epsc", "ngcc", "S16", "xlast", "idn16"))

        def fn_exp_td(eng):
            eng.activation(out=wtmp, in_=tdr, func=AF.Exp)
            return eng.activation(out=eurow[:], in_=tfr, func=AF.Exp)
        ac(fn_exp_td, rd=("rbuf",), wr=("rbuf", "eurow"))

        def fn_wneg(eng):
            eng.tensor_scalar(out=wrow[:], in0=wtmp, scalar1=-1.0, scalar2=None, op0=ALU.mult)
            eng.drain()
            eng.reciprocal(rtmp1[:], tempr[:])
            eng.drain()
            return eng.tensor_scalar(out=rtmp1[:], in0=rtmp1[:], scalar1=-1.0, scalar2=None, op0=ALU.mult)
        dv(fn_wneg, rd=("rbuf", "tempr"), wr=("wrow", "rtmp1"))

        def fn_bcast_mm(eng):
            eng.matmul(pS[:, 0:16], onesf[:], wrow[:], start=True, stop=True)
            eng.matmul(pS[:, 16:32], onesf[:], eurow[:], start=True, stop=True)
            return eng.matmul(pS[:, 32:33], onesf[:], rtmp1[:], start=True, stop=True)
        pe(fn_bcast_mm, rd=("onesf", "wrow", "eurow", "rtmp1"), wr=("pS",))

        def fn_bcast_ev(eng):
            eng.tensor_copy(wbc[:], pS[:, 0:16])
            eng.tensor_copy(eubc[:], pS[:, 16:32])
            return eng.tensor_copy(nrtc[:], pS[:, 32:33])
        dv(fn_bcast_ev, rd=("pS",), wr=("wbc", "eubc", "nrtc"))

        for h0 in range(0, H, 4):
            def fn_tri_a(eng, h0=h0):
                last = None
                for k in range(4):
                    last = eng.activation(out=rbuf[:, 384 + k * 128:512 + k * 128], in_=sc1[:, 0:128],
                                          func=AF.Exp, scale=wbc[:, h0 + k:h0 + k + 1])
                return last
            ac(fn_tri_a, rd=("sc1", "wbc"), wr=("rbuf",))

            def fn_tri_b(eng, h0=h0):
                last = None
                for k in range(4):
                    last = eng.tensor_mul(tri16[:, h0 + k, :], rbuf[:, 384 + k * 128:512 + k * 128],
                                          ek[:, 0:128])
                return last
            dv(fn_tri_b, rd=("rbuf", "ek"), wr=("tri16",))

        def fn_tri_c(eng):
            last = None
            for hh in range(H):
                last = eng.scalar_tensor_tensor(out=tri16[:, hh, :], in0=rbuf[:, 256:384],
                                                scalar=eubc[:, hh:hh + 1], in1=tri16[:, hh, :],
                                                op0=ALU.mult, op1=ALU.add)
            return last
        dv(fn_tri_c, rd=("rbuf", "eubc", "tri16"), wr=("tri16",))

        def fn_vecs(eng):
            for hh in range(H):
                eng.activation(out=dvec16[0:1, hh * 128:(hh + 1) * 128], in_=rbuf[0:1, 0:128],
                               func=AF.Exp, scale=wrow[0:1, hh:hh + 1])
                eng.activation(out=rvec16[:, hh:hh + 1], in_=rbuf[:, 128:129],
                               func=AF.Exp, scale=wbc[:, hh:hh + 1])
            return eng.activation(out=eL16[:], in_=wrow[:], func=AF.Exp, scale=128.0)
        ac(fn_vecs, rd=("rbuf", "wrow", "wbc"), wr=("dvec16", "rvec16", "eL16"))

        for tl in range(NTILE):
            mWk, mWv, mWr, mWo = 28 * tl, 28 * tl + 1, 28 * tl + 2, 28 * tl + 3

            def emit_C1(cc):
                a0 = cc * 128

                def fn_yt(eng, cc=cc):
                    last = None
                    for j in range(8):
                        last = eng.transpose(pT[:, j * 128:(j + 1) * 128],
                                             yTM[:, cc, j * 128:(j + 1) * 128], idn16[:])
                    return last
                pe(fn_yt, rd=(f"yTM{cc}", "idn16"), wr=("pT",))

                def fn_yev(eng, a0=a0):
                    return eng.activation(out=yFM[:, :, a0:a0 + 128],
                                          in_=pT[:].rearrange("p (c n) -> p c n", n=128),
                                          func=AF.Copy)
                ac(fn_yev, rd=("pT",), wr=(YFM[cc],))

            def emit_A0(cc):
                a0 = cc * 128

                def fn_mix(eng, a0=a0, cc=cc):
                    x3 = xfmt[:, :, a0:a0 + 128]
                    if cc > 0:
                        eng.tensor_sub(xr16[:], x3, xfmt[:, :, a0 - 1:a0 + 127])
                    else:
                        eng.tensor_sub(xr16[:, :, 1:128], xfmt[:, :, 1:128], xfmt[:, :, 0:127])
                        eng.tensor_sub(xr16[:, :, 0:1], xfmt[:, :, 0:1], xlast[:])
                    eng.drain()
                    for jc in range(8):
                        eng.tensor_scalar(out=xk16[:, jc, :], in0=xr16[:, jc, :],
                                          scalar1=mk1c[:, jc:jc + 1], scalar2=None, op0=ALU.mult)
                        eng.tensor_scalar(out=xv16[:, jc, :], in0=xr16[:, jc, :],
                                          scalar1=mv1c[:, jc:jc + 1], scalar2=None, op0=ALU.mult)
                    eng.drain()
                    eng.tensor_add(xk16[:], xk16[:], x3)
                    eng.tensor_add(xv16[:], xv16[:], x3)
                    eng.drain()
                    for jc in range(8):
                        eng.tensor_scalar(out=xr16[:, jc, :], in0=xr16[:, jc, :],
                                          scalar1=mr1c[:, jc:jc + 1], scalar2=None, op0=ALU.mult)
                    eng.drain()
                    return eng.tensor_add(xr16[:], xr16[:], x3)
                if cc <= 1:
                    deps = ARAA
                elif cc == 2:
                    deps = ARAA + ARAB
                elif cc == 3:
                    deps = ARAB
                elif cc == 4:
                    deps = ARAB + ARBH
                else:
                    deps = ARBH
                dv(fn_mix, rd=deps + ("xlast", "mk1c", "mv1c", "mr1c"), wr=("xk16", "xv16", "xr16"))

            def emit_A1(cc, tl=tl, mWk=mWk, mWv=mWv):
                def fn_kproj(eng, m=mWk):
                    w = wslotap(m)
                    last = None
                    for jc in range(8):
                        st, sp = jc == 0, jc == 7
                        eng.matmul(pA[:, 0:512], xk16[:, jc, :], w[:, jc, 0:512], start=st, stop=sp)
                        last = eng.matmul(pA[:, 512:1024], xk16[:, jc, :], w[:, jc, 512:1024], start=st, stop=sp)
                    return last
                pe(fn_kproj, rd=(f"slot{mWk % NSLOT}", "xk16"), wr=("pA0", "pA1"))
                if cc == NCH - 1 and mWk + NSLOT < NW:
                    emit_wdma(mWk + NSLOT)

                def fn_ek(eng):
                    return eng.activation(out=ek[:], in_=pA[:], func=AF.Exp)
                ac(fn_ek, rd=("pA0", "pA1"), wr=("ek",))

                def fn_vproj(eng, m=mWv):
                    w = wslotap(m)
                    last = None
                    for jc in range(8):
                        st, sp = jc == 0, jc == 7
                        eng.matmul(pB[:, 0:512], xv16[:, jc, :], w[:, jc, 0:512], start=st, stop=sp)
                        last = eng.matmul(pB[:, 512:1024], xv16[:, jc, :], w[:, jc, 512:1024], start=st, stop=sp)
                    return last
                pe(fn_vproj, rd=(f"slot{mWv % NSLOT}", "xv16"), wr=("pB0", "pB1"))
                if cc == NCH - 1 and mWv + NSLOT < NW:
                    emit_wdma(mWv + NSLOT)

            def emit_C2(cc):
                a0 = cc * 128

                def fn_route(eng, a0=a0):
                    last = None
                    for dc in range(8):
                        last = eng.matmul(pS[:, 16:24], yFM[:, dc, a0:a0 + 128], wrf16[:, dc, :],
                                          start=(dc == 0), stop=(dc == 7))
                    return last
                pe(fn_route, rd=(YFM[cc], "wrf16"), wr=("pS",))

            def emit_C2b(cc):
                def fn_zev(eng, cc=cc):
                    return eng.activation(out=zroute[:, cc, :], in_=pS[:, 16:24], func=AF.Copy)
                ac(fn_zev, rd=("pS",), wr=("zroute",))

            def emit_B1a(cc, tl=tl):
                def fn_wkv(eng, q=cc % 2):
                    last = None
                    for hh in range(H):
                        bank, b0 = hreg(hh)
                        eng.matmul(bank[:, b0:b0 + 65], tri16[:, hh, :], rhs65[:, q, hh, :],
                                   start=True, stop=False)
                        last = eng.matmul(bank[:, b0:b0 + 65], dvec16[0:1, hh * 128:(hh + 1) * 128],
                                          S16[0:1, hh * 65:(hh + 1) * 65], start=False, stop=True)
                    return last
                pe(fn_wkv, rd=("tri16", "dvec16", f"rhs65{cc % 2}", "S16"), wr=("pN0", "pN1", "pS"))

            def emit_B1b(cc):
                def fn_den2(eng, q=cc % 2):
                    eng.tensor_mul(rr[0][:, 0:16], eks[:, q, :], eubc[:])
                    eng.drain()
                    eng.tensor_scalar(out=rr[1][:, 0:16], in0=rr[0][:, 0:16], scalar1=1e-6,
                                      scalar2=None, op0=ALU.max)
                    eng.drain()
                    eng.tensor_sub(rr[1][:, 0:16], rr[1][:, 0:16], rr[0][:, 0:16])
                    eng.drain()
                    dA = pN[:, 0:455].rearrange("p (h c) -> p h c", c=65)[:, :, 64:65]
                    dB = pN[:, 512:967].rearrange("p (h c) -> p h c", c=65)[:, :, 64:65]
                    dC = pS[:, 64:194].rearrange("p (h c) -> p h c", c=65)[:, :, 64:65]
                    eng.tensor_add(rr[1][:, 0:7].unsqueeze(2), rr[1][:, 0:7].unsqueeze(2), dA)
                    eng.tensor_add(rr[1][:, 7:14].unsqueeze(2), rr[1][:, 7:14].unsqueeze(2), dB)
                    eng.tensor_add(rr[1][:, 14:16].unsqueeze(2), rr[1][:, 14:16].unsqueeze(2), dC)
                    eng.drain()
                    return eng.reciprocal(rr[2][:, 0:16], rr[1][:, 0:16])
                dv(fn_den2, rd=(f"eks{cc % 2}", "eubc", "pN0", "pN1", "pS"), wr=("rr0", "rr1", "rr2"))

                def fn_num(eng):
                    nA = pN[:, 0:455].rearrange("p (h c) -> p h c", c=65)[:, :, 0:64]
                    nB = pN[:, 512:967].rearrange("p (h c) -> p h c", c=65)[:, :, 0:64]
                    nC = pS[:, 64:194].rearrange("p (h c) -> p h c", c=65)[:, :, 0:64]
                    s3 = sc1[:].rearrange("p (h d) -> p h d", d=64)
                    rb = lambda a, b: rr[2][:, a:b].unsqueeze(2).broadcast_to([128, b - a, 64])
                    eng.tensor_mul(s3[:, 0:7], nA, rb(0, 7))
                    eng.tensor_mul(s3[:, 7:14], nB, rb(7, 14))
                    eng.tensor_mul(s3[:, 14:16], nC, rb(14, 16))
                    eng.drain()
                    return eng.tensor_mul(rw16[:], sc1[:], rbuf[:])
                dv(fn_num, rd=("pN0", "pN1", "pS", "rr2", "rbuf"), wr=("sc1", "rw16"))

            def emit_A2(cc, tl=tl, mWr=mWr):
                def fn_rproj(eng, m=mWr):
                    w = wslotap(m)
                    last = None
                    for jc in range(8):
                        st, sp = jc == 0, jc == 7
                        eng.matmul(pA[:, 0:512], xr16[:, jc, :], w[:, jc, 0:512], start=st, stop=sp)
                        last = eng.matmul(pA[:, 512:1024], xr16[:, jc, :], w[:, jc, 512:1024], start=st, stop=sp)
                    return last
                pe(fn_rproj, rd=(f"slot{mWr % NSLOT}", "xr16"), wr=("pA0", "pA1"))
                if cc == NCH - 1 and mWr + NSLOT < NW:
                    emit_wdma(mWr + NSLOT)

            def emit_B2(cc):
                def fn_state(eng, q=cc % 2):
                    last = None
                    for hh in range(H):
                        bank, b0 = hreg(hh)
                        eng.matmul(bank[0:1, b0:b0 + 65], rvec16[:, hh:hh + 1], rhs65[:, q, hh, :],
                                   start=True, stop=False)
                        last = eng.matmul(bank[0:1, b0:b0 + 65], eL16[0:1, hh:hh + 1],
                                          S16[0:1, hh * 65:(hh + 1) * 65], start=False, stop=True)
                    return last
                pe(fn_state, rd=("rvec16", "eL16", f"rhs65{cc % 2}", "S16"), wr=("pN0", "pN1", "pS"))

            def emit_A3(cc):
                def fn_rchain(eng):
                    eng.activation(out=rbuf[:], in_=pA[:], func=AF.Exp, scale=-1.0)
                    eng.drain()
                    eng.activation(out=sc2[:], in_=rbuf[:], func=AF.Ln, bias=onec[:, 0:1])
                    eng.drain()
                    return eng.activation(out=rbuf[:], in_=sc2[:], func=AF.Exp, scale=-1.0)
                ac(fn_rchain, rd=("pA0", "pA1", "onec"), wr=("rbuf", "sc2"))

            def emit_B3(cc):
                def fn_rwt(eng):
                    last = None
                    for j in range(8):
                        last = eng.transpose(pT[:, j * 128:(j + 1) * 128], rw16[:, j * 128:(j + 1) * 128],
                                             idn16[:])
                    return last
                pe(fn_rwt, rd=("rw16", "idn16"), wr=("pT",))

                def fn_rwev(eng):
                    return eng.activation(out=rw16[:].rearrange("p (c n) -> p c n", n=128),
                                          in_=pT[:].rearrange("p (c n) -> p c n", n=128),
                                          func=AF.Copy)
                ac(fn_rwev, rd=("pT", "rw16"), wr=("rw16",))

            def emit_B3b(cc, tl=tl, mWo=mWo):
                def fn_wo(eng, m=mWo):
                    w = wslotap(m)
                    last = None
                    for jc in range(8):
                        st, sp = jc == 0, jc == 7
                        eng.matmul(pB[:, 0:512], rw16[:, jc * 128:(jc + 1) * 128], w[:, jc, 0:512],
                                   start=st, stop=sp)
                        last = eng.matmul(pB[:, 512:1024], rw16[:, jc * 128:(jc + 1) * 128],
                                          w[:, jc, 512:1024], start=st, stop=sp)
                    return last
                pe(fn_wo, rd=(f"slot{mWo % NSLOT}", "rw16"), wr=("pB0", "pB1"))
                if cc == NCH - 1 and mWo + NSLOT < NW:
                    emit_wdma(mWo + NSLOT)

            def emit_B4a(cc):
                def fn_sevac(eng):
                    eng.activation(out=S16[0:1, 0:455], in_=pN[0:1, 0:455], func=AF.Copy)
                    eng.activation(out=S16[0:1, 455:910], in_=pN[0:1, 512:967], func=AF.Copy)
                    return eng.activation(out=S16[0:1, 910:1040], in_=pS[0:1, 64:194], func=AF.Copy)
                ac(fn_sevac, rd=("pN0", "pN1", "pS"), wr=("S16",))

            def emit_B4(cc, tl=tl):
                g = tl * NCH + cc

                def fn_xres(eng, g=g):
                    return eng.tensor_add(sc1[:], xtmb[:, g % 2, :], pB[:])
                dv(fn_xres, rd=(f"xtmb{g % 2}", "pB0", "pB1"), wr=("sc1",))
                if g + 2 < T // 128:
                    def fn_xtm2(eng, g2=g + 2):
                        return eng.dma_start(out=xtmb[:, g2 % 2, :], in_=d_xtm[g2 * 128:(g2 + 1) * 128, :])
                    S.emit("sync", fn_xtm2, rd=(), wr=(f"xtmb{g % 2}",), dma_sem=f"dT{g % 2}")

                def fn_lnC(eng, cc=cc):
                    eng.activation(out=sc2[:], in_=sc1[:], func=AF.Identity, accum_out=msum[:])
                    eng.drain()
                    eng.activation(out=negm[:], in_=msum[:], func=AF.Copy, scale=-1.0 / 1024)
                    eng.drain()
                    eng.activation(out=sc2[:], in_=sc1[:], func=AF.Square, bias=negm[:, 0:1],
                                   accum_out=vsum[:])
                    eng.drain()
                    eng.activation(out=tln[:], in_=vsum[:], func=AF.Ln, scale=1.0 / 1024, bias=epsc[:, 0:1])
                    eng.activation(out=rstd[:], in_=tln[:], func=AF.Exp, scale=-0.5)
                    eng.drain()
                    eng.activation(out=nmr[:], in_=negm[:], func=AF.Identity, scale=rstd[:, 0:1])
                    eng.drain()
                    return eng.activation(out=yTM[:, cc, :], in_=sc1[:], func=AF.Identity,
                                          scale=rstd[:, 0:1], bias=nmr[:, 0:1])
                ac(fn_lnC, rd=("sc1", "epsc"), wr=("sc2", "msum", "negm", "vsum", "tln", "rstd",
                                                   "nmr", f"yTM{cc}"))

            def emit_A4(cc):
                def fn_ekv(eng, q=cc % 2):
                    v3 = lambda t: t[:].rearrange("p (h d) -> p h d", d=64)
                    eng.tensor_mul(rhs65[:, q, :, 0:64], v3(ek), v3(pB))
                    eng.drain()
                    eng.reduce_sum(eks[:, q, :], v3(ek), axis=AX.X)
                    eng.drain()
                    return eng.tensor_copy(rhs65[:, q, :, 64:65], eks[:, q, :].unsqueeze(2))
                dv(fn_ekv, rd=("ek", "pB0", "pB1"), wr=(f"rhs65{cc % 2}", f"eks{cc % 2}"))

            if tl == 0:
                emit_A0(0)
                emit_A1(0)
                emit_A4(0)
            for it in range(NCH + 2):
                if 0 < it < NCH:
                    emit_A1(it)
                    emit_A4(it)
                if 1 <= it <= NCH:
                    emit_B1a(it - 1)
                    if it == 1 and tl > 0:
                        emit_A2(0)
                        emit_A3(0)
                    emit_B1b(it - 1)
                if it < NCH and not (it == 0 and tl > 0):
                    emit_A2(it)
                if 1 <= it <= NCH:
                    emit_B3(it - 1)
                if it < NCH and not (it == 0 and tl > 0):
                    emit_A3(it)
                if 1 <= it <= NCH:
                    emit_B2(it - 1)
                    emit_B4a(it - 1)
                if it + 1 < NCH and not (it == 0 and tl > 0):
                    emit_A0(it + 1)
                if 2 <= it:
                    emit_C1(it - 2)
                if 1 <= it <= NCH:
                    emit_B3b(it - 1)
                if 2 <= it:
                    emit_C2(it - 2)
                if 1 <= it <= NCH:
                    emit_B4(it - 1)
                if 2 <= it:
                    emit_C2b(it - 2)
            def fn_xl(eng):
                return eng.tensor_copy(xlast[:], xfmt[:, :, 1023:1024])
            dv(fn_xl, rd=("arB7",), wr=("xlast",))
            emit_A0_last = emit_A0
            emit_A1_last = emit_A1
            emit_A4_last = emit_A4

            def fn_rt_act(eng):
                zv = zroute[:].rearrange("p c e -> p (c e)")
                eng.activation(out=rr[0][:], in_=zv, func=AF.Exp, scale=nrtc[:, 0:1])
                eng.activation(out=rr[1][:], in_=rr[0][:], func=AF.Ln, bias=onec[:, 0:1])
                eng.activation(out=rr[0][:], in_=rr[1][:], func=AF.Exp, scale=-1.0)
                eng.activation(out=rr[1][:], in_=rr[0][:], func=AF.Abs, bias=ngcc[:, 0:1])
                eng.activation(out=rr[2][:], in_=rr[1][:], func=AF.Exp, scale=-10.0)
                return eng.activation(out=rr[3][:], in_=rr[1][:], func=AF.Exp, scale=-1.0 / 0.3)
            ac(fn_rt_act, rd=("zroute", "nrtc", "onec", "ngcc"), wr=("rr0", "rr1", "rr2", "rr3"))

            def fn_rt_dve(eng):
                v88 = lambda t: t[:].rearrange("p (c e) -> p c e", e=8)
                b8 = lambda t: t[:].unsqueeze(2).broadcast_to([128, 8, 8])
                eng.tensor_scalar(out=rr[4][:], in0=rr[0][:], scalar1=GL, scalar2=None, op0=ALU.is_ge)
                eng.drain()
                eng.tensor_scalar(out=rr[1][:], in0=rr[0][:], scalar1=GU, scalar2=None, op0=ALU.is_le)
                eng.drain()
                eng.tensor_mul(rr[4][:], rr[4][:], rr[1][:])
                eng.drain()
                eng.tensor_mul(rr[2][:], rr[2][:], rr[4][:])
                eng.drain()
                eng.reduce_sum(rs8a[:], v88(rr[2]), axis=AX.X)
                eng.drain()
                eng.tensor_scalar(out=rne[:], in0=rs8a[:], scalar1=1e-8, scalar2=None, op0=ALU.is_lt)
                eng.drain()
                fb = rr[3][:].rearrange("p (c e) -> p c e", e=8)
                e4a = fb.unsqueeze(2).broadcast_to([128, 8, 8, 8])
                e4b = fb.unsqueeze(3).broadcast_to([128, 8, 8, 8])
                cnt4 = sc2[:, 0:512].rearrange("p (c e f) -> p c e f", e=8, f=8)
                eng.tensor_tensor(cnt4, e4a, e4b, ALU.is_gt)
                eng.drain()
                eng.reduce_sum(v88(rr[4]), cnt4, axis=AX.X)
                eng.drain()
                eng.tensor_scalar(out=rr[4][:], in0=rr[4][:], scalar1=1.5, scalar2=None, op0=ALU.is_le)
                eng.drain()
                eng.tensor_mul(rr[4][:], rr[3][:], rr[4][:])
                eng.drain()
                eng.reduce_sum(rs8b[:], v88(rr[4]), axis=AX.X)
                eng.drain()
                eng.tensor_scalar(out=rs8b[:], in0=rs8b[:], scalar1=1e-8, scalar2=None, op0=ALU.max)
                eng.drain()
                eng.reciprocal(rs8a[:], rs8b[:])
                eng.drain()
                eng.tensor_mul(v88(rr[4]), v88(rr[4]), b8(rs8a))
                eng.drain()
                eng.tensor_sub(rr[4][:], rr[4][:], rr[2][:])
                eng.drain()
                eng.tensor_mul(v88(rr[4]), v88(rr[4]), b8(rne))
                eng.drain()
                eng.tensor_add(rr[2][:], rr[2][:], rr[4][:])
                eng.drain()
                eng.reduce_sum(rs8b[:], v88(rr[2]), axis=AX.X)
                eng.drain()
                eng.tensor_scalar(out=rs8b[:], in0=rs8b[:], scalar1=1e-8, scalar2=None, op0=ALU.max)
                eng.drain()
                eng.reciprocal(rs8a[:], rs8b[:])
                eng.drain()
                return eng.tensor_mul(wts[:], v88(rr[2]), b8(rs8a))
            dv(fn_rt_dve, rd=("rr0", "rr1", "rr2", "rr3"),
               wr=("rr1", "rr2", "rr4", "sc2", "rs8a", "rs8b", "rne", "wts"))

            def emit_mixb(jc):
                def fn_mixb(eng, jc=jc):
                    x3 = xfmt[:, jc, 0:128]
                    eng.tensor_sub(xr16[:, jc, 1:128], xfmt[:, jc, 1:128],
                                   xfmt[:, jc, 0:127])
                    eng.tensor_sub(xr16[:, jc, 0:1], xfmt[:, jc, 0:1], xlast[:, jc, :])
                    eng.drain()
                    eng.tensor_scalar(out=xk16[:, jc, :], in0=xr16[:, jc, :],
                                      scalar1=mk1c[:, jc:jc + 1], scalar2=None, op0=ALU.mult)
                    eng.tensor_scalar(out=xv16[:, jc, :], in0=xr16[:, jc, :],
                                      scalar1=mv1c[:, jc:jc + 1], scalar2=None, op0=ALU.mult)
                    eng.drain()
                    eng.tensor_add(xk16[:, jc, :], xk16[:, jc, :], x3)
                    eng.tensor_add(xv16[:, jc, :], xv16[:, jc, :], x3)
                    eng.drain()
                    eng.tensor_scalar(out=xr16[:, jc, :], in0=xr16[:, jc, :],
                                      scalar1=mr1c[:, jc:jc + 1], scalar2=None, op0=ALU.mult)
                    eng.drain()
                    return eng.tensor_add(xr16[:, jc, :], xr16[:, jc, :], x3)
                dv(fn_mixb, rd=(f"arAA{jc}", "xlast", "mk1c", "mv1c", "mr1c"),
                   wr=("xk16", "xv16", "xr16"))

            def emit_stageE(cc, tl=tl):
                g = tl * NCH + cc
                xb, scr, ng, rs = (sc1, sc2, negm, rstd) if cc % 2 == 0 else (rbuf, ek, negmB, rstdB)
                xbr, scrr = ("sc1", "sc2") if cc % 2 == 0 else ("rbuf", "ek")
                ngr, rsr = ("negm", "rstd") if cc % 2 == 0 else ("negmB", "rstdB")

                def fn_xres2(eng, cc=cc, xb=xb):
                    return eng.tensor_add(xb[:], yTM[:, cc, :], accv[:, cc, :])
                dv(fn_xres2, rd=(f"yTM{cc}", f"arAA{cc}", f"arAB{cc}", f"arB{cc}"), wr=(xbr,))

                def fn_lnE(eng, xb=xb, scr=scr, ng=ng, rs=rs):
                    eng.activation(out=scr[:], in_=xb[:], func=AF.Identity, accum_out=msum[:])
                    eng.drain()
                    eng.activation(out=ng[:], in_=msum[:], func=AF.Copy, scale=-1.0 / 1024)
                    eng.drain()
                    eng.activation(out=scr[:], in_=xb[:], func=AF.Square, bias=ng[:, 0:1],
                                   accum_out=vsum[:])
                    eng.drain()
                    eng.activation(out=tln[:], in_=vsum[:], func=AF.Ln, scale=1.0 / 1024, bias=epsc[:, 0:1])
                    return eng.activation(out=rs[:], in_=tln[:], func=AF.Exp, scale=-0.5)
                ac(fn_lnE, rd=(xbr, "epsc"), wr=(scrr, "msum", ngr, "vsum", "tln", rsr))

                def fn_out(eng, xb=xb, ng=ng, rs=rs):
                    eng.activation(out=nmr[:], in_=ng[:], func=AF.Identity, scale=rs[:, 0:1])
                    eng.drain()
                    return eng.activation(out=outb[:], in_=xb[:], func=AF.Identity,
                                          scale=rs[:, 0:1], bias=nmr[:, 0:1])
                ac(fn_out, rd=(xbr, ngr, rsr), wr=("outb", "nmr"))

                def fn_odma(eng, g=g):
                    return eng.dma_start(out=d_out[g * 128:(g + 1) * 128, :], in_=outb[:])
                S.emit("sync" if cc < 4 else "scalar", fn_odma, rd=("outb",), wr=(), dma_sem="dO")

            def emit_tail(cc, tl=tl):
                emit_stageE(cc)
                if tl + 1 < NTILE:
                    t0n = (tl + 1) * TTILE
                    if cc < 4:
                        def fn_xfmb(eng, jc=cc, t0=t0n):
                            return eng.dma_start(out=xfmt[:, jc, 0:512],
                                                 in_=d_xfm[jc * 128:(jc + 1) * 128, t0:t0 + 512])
                        S.emit("scalar", fn_xfmb, rd=(),
                               wr=(f"arAA{cc}", f"arAB{cc}"), dma_sem=f"dX{cc}")

                        def fn_xfmb2(eng, jc=cc, t0=t0n):
                            return eng.dma_start(out=xfmt[:, jc, 512:1024],
                                                 in_=d_xfm[jc * 128:(jc + 1) * 128,
                                                           t0 + 512:t0 + 1024])
                        S.emit("sync", fn_xfmb2, rd=(), wr=(f"arB{cc}",), dma_sem=f"dXB{cc}")
                    else:
                        def fn_xfmaa(eng, jc=cc, t0=t0n):
                            return eng.dma_start(out=xfmt[:, jc, 0:256],
                                                 in_=d_xfm[jc * 128:(jc + 1) * 128, t0:t0 + 256])
                        S.emit("sync", fn_xfmaa, rd=(), wr=(f"arAA{cc}",), dma_sem=f"dX{cc}")
                    if cc >= 1:
                        emit_mixb(cc - 1)

            def emit_tail2(tl=tl):
                if tl + 1 >= NTILE:
                    return
                t0n = (tl + 1) * TTILE
                for jc in range(4, 8):
                    def fn_xfmab(eng, jc=jc, t0=t0n):
                        return eng.dma_start(out=xfmt[:, jc, 256:512],
                                             in_=d_xfm[jc * 128:(jc + 1) * 128, t0 + 256:t0 + 512])
                    S.emit("sync", fn_xfmab, rd=(), wr=(f"arAB{jc}",), dma_sem=f"dXA{jc}")
                for jc in range(4, 8):
                    def fn_xfmb2(eng, jc=jc, t0=t0n):
                        return eng.dma_start(out=xfmt[:, jc, 512:1024],
                                             in_=d_xfm[jc * 128:(jc + 1) * 128, t0 + 512:t0 + 1024])
                    S.emit("sync", fn_xfmb2, rd=(), wr=(f"arB{jc}",), dma_sem=f"dXB{jc}")

            for e in range(E):
                mG, mU, mD = 28 * tl + 4 + 3 * e, 28 * tl + 5 + 3 * e, 28 * tl + 6 + 3 * e
                for half in range(2):
                    t0 = half * 512
                    ycols = tuple(YFM[half * 4 + i] for i in range(4))
                    for ic in range(8):
                        p = ic % 2
                        gb = pA[:, 0:512] if p == 0 else pA[:, 512:1024]
                        ub = pN[:, 0:512] if p == 0 else pN[:, 512:1024]

                        def fn_gate(eng, m=mG, ic=ic, t0=t0, gb=gb):
                            w = wslotap(m)
                            last = None
                            for dc in range(8):
                                last = eng.matmul(gb, w[:, dc, ic * 128:(ic + 1) * 128],
                                                  yFM[:, dc, t0:t0 + 512],
                                                  start=(dc == 0), stop=(dc == 7))
                            return last
                        pe(fn_gate, rd=(f"slot{mG % NSLOT}",) + ycols, wr=(f"pA{p}",))
                        if half == 1 and ic == 7 and mG + NSLOT < NW:
                            emit_wdma(mG + NSLOT)

                        def fn_silu(eng, p=p, gb=gb):
                            return eng.activation(out=g_s[:, p, :], in_=gb, func=AF.Silu)
                        ac(fn_silu, rd=(f"pA{p}",), wr=(f"g_s{p}",))

                        def fn_up(eng, m=mU, ic=ic, t0=t0, ub=ub):
                            w = wslotap(m)
                            last = None
                            for dc in range(8):
                                last = eng.matmul(ub, w[:, dc, ic * 128:(ic + 1) * 128],
                                                  yFM[:, dc, t0:t0 + 512],
                                                  start=(dc == 0), stop=(dc == 7))
                            return last
                        pe(fn_up, rd=(f"slot{mU % NSLOT}",) + ycols, wr=(f"pN{p}",))
                        if half == 1 and ic == 7 and mU + NSLOT < NW:
                            emit_wdma(mU + NSLOT)

                        def fn_h(eng, ic=ic, p=p, ub=ub):
                            return eng.tensor_mul(h16[:, ic, :], g_s[:, p, :], ub)
                        dv(fn_h, rd=(f"g_s{p}", f"pN{p}"), wr=(f"h{ic}",))

                    for tc in range(4):
                        cc_g = half * 4 + tc
                        bank, brs = (pA, ("pA0", "pA1")) if tc % 2 == 0 else (pB, ("pB0", "pB1"))

                        def fn_down_a(eng, m=mD, tc=tc, bank=bank):
                            w = wslotap(m)
                            last = None
                            for ic in range(7):
                                st = ic == 0
                                eng.matmul(bank[:, 0:512], h16[:, ic, tc * 128:(tc + 1) * 128],
                                           w[:, ic, 0:512], start=st, stop=False)
                                last = eng.matmul(bank[:, 512:1024], h16[:, ic, tc * 128:(tc + 1) * 128],
                                                  w[:, ic, 512:1024], start=st, stop=False)
                            return last
                        pe(fn_down_a, rd=(f"slot{mD % NSLOT}",) + tuple(f"h{i}" for i in range(7)),
                           wr=brs)

                        def fn_down_b(eng, m=mD, tc=tc, bank=bank):
                            w = wslotap(m)
                            eng.matmul(bank[:, 0:512], h16[:, 7, tc * 128:(tc + 1) * 128],
                                       w[:, 7, 0:512], start=False, stop=True)
                            return eng.matmul(bank[:, 512:1024], h16[:, 7, tc * 128:(tc + 1) * 128],
                                              w[:, 7, 512:1024], start=False, stop=True)
                        pe(fn_down_b, rd=(f"slot{mD % NSLOT}", "h7"), wr=brs)
                        if half == 1 and tc == 3 and mD + NSLOT < NW:
                            emit_wdma(mD + NSLOT)

                        def fn_acc(eng, cc_g=cc_g, e=e, bank=bank):
                            if e == 0:
                                return eng.tensor_scalar(out=accv[:, cc_g, :], in0=bank[:],
                                                         scalar1=wts[:, cc_g, e:e + 1], scalar2=None,
                                                         op0=ALU.mult)
                            return eng.scalar_tensor_tensor(out=accv[:, cc_g, :], in0=bank[:],
                                                            scalar=wts[:, cc_g, e:e + 1],
                                                            in1=accv[:, cc_g, :],
                                                            op0=ALU.mult, op1=ALU.add)
                        dv(fn_acc, rd=brs + ("wts", f"arAA{cc_g}", f"arAB{cc_g}", f"arB{cc_g}"),
                           wr=(f"arAA{cc_g}", f"arAB{cc_g}", f"arB{cc_g}"))

                        if e == E - 1 and cc_g >= 1:
                            emit_tail(cc_g - 1)
            emit_tail(7)
            emit_tail2()
            if tl + 1 < NTILE:
                emit_mixb(7)
                emit_A1_last(0, mWk=28 * (tl + 1), mWv=28 * (tl + 1) + 1)
                emit_A0_last(1)
                emit_A4_last(0)


        def fn_fin(eng):
            return None
        S.emit("sync", fn_fin, rd=(), wr=(), extra=[("dO", 16 * (T // 128))], bare=True)

        with nc.Block() as block:
            def mk(eng_name):
                def body(engine):
                    waited = defaultdict(int)
                    for (en, fn, waits, inc, n) in S.steps:
                        if en != eng_name:
                            continue
                        for (sname, v) in waits:
                            if v > waited[sname]:
                                engine.wait_ge(sems[sname], v)
                                waited[sname] = v
                        last = fn(engine)
                        if inc is not None and last is not None:
                            last.then_inc(sems[inc], n)
                        elif inc is not None:
                            raise RuntimeError(f"step on {eng_name} returned no instruction")
                return body
            block.sync(mk("sync"))
            block.gpsimd(mk("gpsimd"))
            block.vector(mk("vector"))
            block.scalar(mk("scalar"))
            block.tensor(mk("tensor"))

    return nc


def kernel(**inp):
    f = np.float32
    x = np.asarray(inp["L0_attn_x"], f)
    wkT = np.ascontiguousarray(np.asarray(inp["L0_attn_Wk"], f).T)
    wvT = np.ascontiguousarray(np.asarray(inp["L0_attn_Wv"], f).T)
    wrT = np.ascontiguousarray(np.asarray(inp["L0_attn_Wr"], f).T)
    woT = np.ascontiguousarray(np.asarray(inp["L0_attn_Wo"], f).T)
    wg = np.asarray(inp["L0_ffn_Wg"], f)
    wu = np.asarray(inp["L0_ffn_Wu"], f)
    wd = np.asarray(inp["L0_ffn_Wd"], f)
    wrf = np.ascontiguousarray(np.asarray(inp["L0_ffn_Wr"], f).reshape(8, 128, 8).transpose(1, 0, 2))
    mk_ = np.ascontiguousarray(np.asarray(inp["L0_attn_mix_k"], f).reshape(8, 128).T)
    mv_ = np.ascontiguousarray(np.asarray(inp["L0_attn_mix_v"], f).reshape(8, 128).T)
    mr_ = np.ascontiguousarray(np.asarray(inp["L0_attn_mix_r"], f).reshape(8, 128).T)
    td = np.asarray(inp["L0_attn_time_decay"], f).reshape(1, 16)
    tf = np.asarray(inp["L0_attn_time_first"], f).reshape(1, 16)
    temp = np.asarray(inp["L0_ffn_temperature"], f).reshape(1, 1)

    s = np.arange(128, dtype=f)
    dmat = (s[None, :] - 1.0 - s[:, None]).astype(f)
    dmask = (dmat >= 0).astype(f)
    ior = s.reshape(1, 128)
    ioc = (127.0 - s).reshape(128, 1)
    idn = np.eye(128, dtype=f)

    nc = build_nc()
    in_maps = []
    for b in range(NCORE):
        m = {"xfm": np.ascontiguousarray(x[b].T), "xtm": np.ascontiguousarray(x[b]),
             "wkT": wkT, "wvT": wvT, "wrT": wrT, "woT": woT,
             "wrf": wrf, "mk": mk_, "mv": mv_, "mr": mr_,
             "td": td, "tf": tf, "temp": temp,
             "dmat": dmat, "dmask": dmask, "ior": ior, "ioc": ioc, "idn": idn}
        for e in range(E):
            m[f"wg{e}"] = wg[e]
            m[f"wu{e}"] = wu[e]
            m[f"wd{e}"] = wd[e]
        in_maps.append(m)

    res = run_bass_kernel_spmd(nc, in_maps, core_ids=list(range(NCORE)))
    return np.stack([np.asarray(r["out"]) for r in res.results], axis=0)
